# version 28
# baseline (speedup 1.0000x reference)
import numpy as np

import concourse.bass as bass
import concourse.mybir as mybir
from concourse.bass_utils import run_bass_kernel_spmd

F32 = mybir.dt.float32
BF16 = mybir.dt.bfloat16
F16 = mybir.dt.float16
AF = mybir.ActivationFunctionType
ALU = mybir.AluOpType
AX = mybir.AxisListType

S = 2048
DM = 1024
NP = 8
NDC = DM // 128
FF = 4096
NFC = FF // 128
NT = S // 128
NQ = S // 512
GELU_C = 0.044715
GELU_SC = float(np.sqrt(GELU_C))
SIG_SCALE = 2.0 * 0.7978845608028654
EXP_SHIFT = -2.0
EPS = 1e-5

ENGINES = ("tensor", "vector", "scalar", "gpsimd", "sync")


class Ctx:
    def __init__(self, nc):
        self.nc = nc
        self.q = {e: [] for e in ENGINES}
        self.counts = {}
        self.sems = {}
        self.satisfied = {}

    def _sem(self, name):
        if name not in self.sems:
            self.sems[name] = self.nc.alloc_semaphore(name)
            self.counts[name] = 0
        return self.sems[name]

    def tok(self, name):
        self._sem(name)
        return (name, self.counts[name])

    def op(self, eng, fn, reads=(), force=(), inc=None, inc_n=1):
        waits = []
        for (sname, val) in tuple(force):
            if val > 0:
                self._sem(sname)
                waits.append((self.sems[sname], val))
                key = (eng, sname)
                if self.satisfied.get(key, 0) < val:
                    self.satisfied[key] = val
        for item in tuple(reads):
            if item is None:
                continue
            sname, val = item
            if val <= 0:
                continue
            self._sem(sname)
            key = (eng, sname)
            if self.satisfied.get(key, 0) >= val:
                continue
            self.satisfied[key] = val
            waits.append((self.sems[sname], val))
        tokv = None
        sh = None
        if inc is not None:
            s = self._sem(inc)
            self.counts[inc] += inc_n
            tokv = (inc, self.counts[inc])
            self.satisfied[(eng, inc)] = self.counts[inc]
            sh = (s, inc_n)

        def run(e, fn=fn, waits=tuple(waits), sh=sh):
            for (s, v) in waits:
                e.wait_ge(s, v)
            inst = fn(e)
            if sh is not None:
                inst.then_inc(sh[0], sh[1])
        self.q[eng].append(run)
        return tokv


class Bank:
    def __init__(self, t):
        self.t = t
        self.w = None
        self.r = None

    def rw(self):
        return [self.r] if self.r is not None else []


class Arena:
    def __init__(self, nc, n_f32):
        self.t = nc.alloc_sbuf_tensor("arena", [128, n_f32], F32)
        self.n = n_f32
        self.ptr = 0

    def alloc(self, name, shape, dtype):
        elems = 1
        for d in shape[1:]:
            elems *= d
        per32 = (elems + 1) // 2 if dtype in (BF16, F16) else elems
        start = (self.ptr + 15) & ~15
        assert start + per32 <= self.n, f"arena OOM at {name}: {start}+{per32}>{self.n}"
        self.ptr = start + per32
        v = self.t[:, :].bitcast(dtype) if dtype != F32 else self.t[:, :]
        off = start * 2 if dtype in (BF16, F16) else start
        flat = v[0:shape[0], off:off + elems]
        if len(shape) == 3:
            flat = flat.rearrange("p (a b) -> p a b", b=shape[2])
        return flat

    def view_at(self, start, shape, dtype):
        elems = 1
        for d in shape[1:]:
            elems *= d
        v = self.t[:, :].bitcast(dtype) if dtype != F32 else self.t[:, :]
        off = start * 2 if dtype in (BF16, F16) else start
        return v[0:shape[0], off:off + elems]

    def mark(self):
        return self.ptr

    def release(self, m):
        self.ptr = m


def build(debug=(), stop_after=9):

    nc = bass.Bass("TRN2", target_bir_lowering=False)
    I = {}
    I["x"] = nc.dram_tensor("x", [S, DM], F32, kind="ExternalInput").ap()
    I["W_attn"] = nc.dram_tensor("W_attn", [DM, 3 * DM], F32, kind="ExternalInput").ap()
    I["b_attn"] = nc.dram_tensor("b_attn", [3 * DM], F32, kind="ExternalInput").ap()
    I["W_out"] = nc.dram_tensor("W_out", [DM, DM], F32, kind="ExternalInput").ap()
    I["b_out"] = nc.dram_tensor("b_out", [DM], F32, kind="ExternalInput").ap()
    I["w_router"] = nc.dram_tensor("w_router", [DM], F32, kind="ExternalInput").ap()
    I["W1"] = nc.dram_tensor("W1", [DM, FF], F32, kind="ExternalInput").ap()
    I["b1"] = nc.dram_tensor("b1", [FF], F32, kind="ExternalInput").ap()
    I["W2"] = nc.dram_tensor("W2", [FF, DM], F32, kind="ExternalInput").ap()
    I["b2"] = nc.dram_tensor("b2", [DM], F32, kind="ExternalInput").ap()
    out_ap = nc.dram_tensor("out", [S, DM], F32, kind="ExternalOutput").ap()

    dumps = {}

    def mkdump(name, shape):
        if name in debug:
            dumps[name] = nc.dram_tensor("dbg_" + name, shape, F32, kind="ExternalOutput").ap()
            return dumps[name]
        return None

    ctx = Ctx(nc)
    op, tok = ctx.op, ctx.tok
    A = Arena(nc, 53184)

    pe_food = [0]
    pe_pending = []

    def feed(n=1):
        pe_food[0] += n
        while pe_pending and pe_pending[0][0] <= pe_food[0]:
            pe_pending.pop(0)[1]()

    def defer(delay, fn):
        pe_pending.append((pe_food[0] + delay, fn))

    def drain_deferred():
        while pe_pending:
            pe_pending.pop(0)[1]()

    dpair = [nc.alloc_psum_tensor(f"dp{i}", [128, 1024], F32) for i in range(2)]
    banks = [Bank(dpair[0][:, 0:512]), Bank(dpair[0][:, 512:1024]),
             Bank(dpair[1][:, 0:512]), Bank(dpair[1][:, 512:1024])]
    banks += [Bank(nc.alloc_psum_tensor(f"pb{bi}", [128, 512], F32)) for bi in range(4, 8)]

    x_tm = A.alloc("x_tm", [128, NT, DM], F32)
    x_fm = A.alloc("x_fm", [128, NDC, S], F16)
    ident_f = A.alloc("ident_f", [128, 128], F32)
    ones1f = A.alloc("ones1f", [1, 128], F32)
    ones32f = A.alloc("ones32f", [33, 64], F32)
    ones33AB = A.alloc("ones33AB", [33, 128], F32)
    onesAf = A.alloc("onesAf", [1, 128], F32)
    onesBf = A.alloc("onesBf", [1, 128], F32)
    onescol_bf = A.alloc("onescol_bf", [128, 16], F16)
    shift_col = A.alloc("shift_col", [128, 1], F32)
    eps_col = A.alloc("eps_col", [128, 1], F32)
    bqk_cols = A.alloc("bqk_cols", [128, 16], F32)
    b1_cols = A.alloc("b1_cols", [128, 32], F32)
    b1s_cols = A.alloc("b1s_cols", [128, 32], F32)
    b2_bc = A.alloc("b2_bc", [128, DM], F32)
    wr_bc = A.alloc("wr_bc", [128, DM], F32)
    stats_s = A.alloc("stats_s", [128, 16], F32)
    stats_q = A.alloc("stats_q", [128, 16], F32)
    stats_m = A.alloc("stats_m", [128, 16], F32)
    stats_r = A.alloc("stats_r", [128, 16], F32)
    stats_t = A.alloc("stats_t", [128, 16], F32)
    s_cols = A.alloc("s_cols", [128, 16], F32)
    sig_cols = A.alloc("sig_cols", [128, 16], F32)
    flag_cols = A.alloc("flag_cols", [128, 16], F32)
    gate_cols = A.alloc("gate_cols", [128, 16], F32)
    xsq = A.alloc("xsq", [128, DM], F32)
    mask_tri = A.alloc("mask_tri", [128, 128], F16)
    u_cols = A.alloc("u_cols", [128, NDC], F32)
    u_hi = A.alloc("u_hi", [128, NDC], F16)
    u_lo = A.alloc("u_lo", [128, NDC], F16)
    xw_cols = A.alloc("xw_cols", [128, 16], F32)
    aw_cols = A.alloc("aw_cols", [128, 16], F32)
    bw_sw = A.alloc("bw_sw", [128, 2], F32)
    nsw_col = A.alloc("nsw_col", [128, 1], F32)
    aw_row = A.alloc("aw_row", [1, S], F32)
    bs_row = A.alloc("bs_row", [1, 2], F32)
    CSB_ALLOCS_NOTE = True
    TMPE_START = (A.ptr + 15) & ~15
    tmp_e = [A.alloc(f"tmp_e{i}", [128, 512], F32) for i in range(2)]
    CSB_MARK0 = A.mark()
    C_sb = A.alloc("C_sb", [128, NDC, S], F16)
    bv_bc = A.alloc("bv_bc", [128, DM], F32)
    bout_bc = A.alloc("bout_bc", [128, DM], F32)

    pre_mark = A.mark()
    bqk_pack = A.alloc("bqk_pack", [16, 128], F32)
    b1_pack = A.alloc("b1_pack", [32, 128], F32)
    bv_row = A.alloc("bv_row", [1, DM], F32)
    bout_row = A.alloc("bout_row", [1, DM], F32)
    b2_row = A.alloc("b2_row", [1, DM], F32)
    wr_row = A.alloc("wr_row", [1, DM], F32)
    Wo32 = A.alloc("Wo32", [128, NDC // 2, DM], F32)

    x_r = I["x"].rearrange("(a p) d -> p a d", p=128)
    for dst, src in (
        (bqk_pack, I["b_attn"][0:2048].rearrange("(r j) -> r j", j=128)),
        (bv_row, I["b_attn"][2048:3072].rearrange("(a j) -> a j", a=1)),
        (bout_row, I["b_out"].rearrange("(a j) -> a j", a=1)),
        (b2_row, I["b2"].rearrange("(a j) -> a j", a=1)),
        (wr_row, I["w_router"].rearrange("(a j) -> a j", a=1)),
        (b1_pack, I["b1"].rearrange("(r j) -> r j", j=128)),
    ):
        op("sync", lambda e, dst=dst, src=src: e.dma_start(dst[:, :], src), inc="ld_b", inc_n=16)
    t_ldb = tok("ld_b")
    op("sync", lambda e: e.dma_start(x_tm[:, 0:4, :], x_r[:, 0:4, :]),
       inc="ld_x", inc_n=16)
    _ = None
    wo_re = I["W_out"].rearrange("(fc p) o -> p fc o", p=128)
    op("sync", lambda e: e.dma_start(Wo32[:, :, :], wo_re[:, 0:4, :]),
       inc="ld_wo32", inc_n=16)
    t_ldwo32 = tok("ld_wo32")
    for g in range(1, 4):
        op("sync", lambda e, g=g: e.dma_start(x_tm[:, 4 * g:4 * g + 4, :], x_r[:, 4 * g:4 * g + 4, :]),
           force=[("ld_x", 16 * g)], inc="ld_x", inc_n=16)
    t_ldx = tok("ld_x")

    tg = op("gpsimd", lambda e: e.memset(ident_f[:, :], 1.0), inc="g_const")
    tg = op("gpsimd", lambda e: e.affine_select(ident_f[:, :], ident_f[:, :], [[-1, 128]],
                                                ALU.is_equal, 0.0, base=0,
                                                channel_multiplier=1),
            force=[tg], inc="g_const")
    op("gpsimd", lambda e: e.memset(ones1f[:, :], 1.0))
    op("gpsimd", lambda e: e.memset(ones32f[:, :], 1.0))
    tg = op("gpsimd", lambda e: e.memset(ones33AB[:, :], 0.0), inc="g_const")
    tg = op("gpsimd", lambda e: e.memset(ones33AB[0:1, 0:64], 1.0), force=[tg],
            inc="g_const")
    tg = op("gpsimd", lambda e: e.memset(ones33AB[32:33, 64:128], 1.0), force=[tg],
            inc="g_const")
    op("gpsimd", lambda e: e.memset(ddR[:, :], 1.0))
    tg = op("gpsimd", lambda e: e.memset(onesAf[:, :], 0.0), inc="g_const")
    tg = op("gpsimd", lambda e: e.memset(onesAf[:, 0:64], 1.0), force=[tg], inc="g_const")
    tg = op("gpsimd", lambda e: e.memset(onesBf[:, :], 0.0), force=[tg], inc="g_const")
    tg = op("gpsimd", lambda e: e.memset(onesBf[:, 64:128], 1.0), force=[tg], inc="g_const")
    op("gpsimd", lambda e: e.memset(onescol_bf[:, :], 1.0))
    tg = op("gpsimd", lambda e: e.memset(mask_tri[:, :], 1.0), force=[tg], inc="g_const")
    tg = op("gpsimd", lambda e: e.affine_select(mask_tri[:, :], mask_tri[:, :], [[1, 128]],
                                                ALU.is_ge, 0.0, base=0,
                                                channel_multiplier=-1),
            force=[tg], inc="g_const")
    op("gpsimd", lambda e: e.memset(shift_col[:, :], EXP_SHIFT))
    t_consts = op("gpsimd", lambda e: e.memset(eps_col[:, :], EPS), force=[tg],
                  inc="g_const")

    b4, b5 = banks[4], banks[5]
    b4.w = op("tensor", lambda e: e.transpose(b4.t[:, 0:16], bqk_pack[:, :], ident_f[0:16, 0:16]),
              reads=[t_ldb, t_consts] + b4.rw(), inc="pe0")
    b4.r = op("vector", lambda e: e.tensor_copy(bqk_cols[:, :], b4.t[:, 0:16]),
              reads=[b4.w], inc="ve_b")
    b5.w = op("tensor", lambda e: e.transpose(b5.t[:, 0:32], b1_pack[:, :], ident_f[0:32, 0:32]),
              reads=[t_ldb, t_consts] + b5.rw(), inc="pe0")
    b5.r = op("vector", lambda e: e.tensor_copy(b1_cols[:, :], b5.t[:, 0:32]),
              reads=[b5.w], inc="ve_b")
    t_b1sc = op("vector", lambda e: e.tensor_scalar(b1s_cols[:, :], b1_cols[:, :], GELU_SC, None, ALU.mult),
                force=[b5.r], inc="ve_b")

    for row_ap, dst in ((bv_row, bv_bc), (bout_row, bout_bc), (b2_row, b2_bc), (wr_row, wr_bc)):
        for half in range(2):
            b = banks[6 + half]
            b.w = op("tensor",
                     lambda e, b=b, half=half, row_ap=row_ap:
                     e.matmul(b.t[:, :], ones1f[0:1, :], row_ap[0:1, 512 * half:512 * half + 512],
                              start=True, stop=True),
                     reads=[t_ldb, t_consts] + b.rw(), inc="pe0")
            b.r = op("vector",
                     lambda e, b=b, half=half, dst=dst:
                     e.tensor_copy(dst[:, 512 * half:512 * half + 512], b.t[:, :]),
                     reads=[b.w], inc="ve_b")
    t_bcasts = tok("ve_b")

    t_m0 = op("vector", lambda e: e.tensor_mul(xsq[0:1, 0:DM], bout_row[:, :], wr_row[:, :]),
              reads=[t_ldb], inc="ve_u")
    op("vector", lambda e: e.reduce_sum(bs_row[:, 0:1], xsq[0:1, 0:DM], AX.X),
       force=[t_m0], inc="ve_u")
    t_bs = op("vector", lambda e: e.reduce_sum(bs_row[:, 1:2], wr_row[:, :], AX.X), inc="ve_u")
    b7 = banks[7]
    b7.w = op("tensor", lambda e: e.matmul(b7.t[:, 0:2], ones1f[0:1, :], bs_row[0:1, :],
                                           start=True, stop=True),
              reads=[t_bs] + b7.rw(), inc="pe0")
    b7.r = op("vector", lambda e: e.tensor_copy(bw_sw[:, :], b7.t[:, 0:2]),
              reads=[b7.w], inc="ve_u")
    t_nsw = op("vector", lambda e: e.tensor_scalar(nsw_col[:, :], bw_sw[:, 1:2], -1.0,
                                                   None, ALU.mult),
               force=[b7.r], inc="ve_u")
    op("vector", lambda e: e.memset(aw_row[:, :], 0.0), inc="ve_u")
    t_rows = tok("ve_u")

    for g in range(4):
        for dc in range(NDC):
            b = banks[(g * NDC + dc) % 4]
            for k in range(4):
                ttt = 4 * g + k
                w = op("tensor",
                       lambda e, b=b, k=k, ttt=ttt, dc=dc:
                       e.transpose(b.t[:, 128 * k:128 * k + 128],
                                   x_tm[:, ttt, 128 * dc:128 * dc + 128], ident_f[:, :]),
                       reads=([("ld_x", 16 * (g + 1)), t_consts] + b.rw()) if k == 0 else (),
                       inc="pe0" if k == 3 else None)
                if k == 3:
                    b.w = w
            b.r = op("vector",
                     lambda e, b=b, dc=dc, g=g:
                     e.tensor_copy(x_fm[:, dc, 512 * g:512 * g + 512], b.t[:, :]),
                     reads=[b.w], inc="ve_xfm")
        if g == 0:
            for fc in range(4):
                t_m = op("vector", lambda e, fc=fc: e.tensor_mul(xsq[:, :], Wo32[:, fc, :],
                                                                 wr_bc[:, :]),
                         reads=[t_ldwo32, t_bcasts] if fc == 0 else (),
                         force=[tok("ve_u")] if fc > 0 else (), inc="ve_u")
                op("vector", lambda e, fc=fc: e.reduce_sum(u_cols[:, fc:fc + 1],
                                                           xsq[:, :], AX.X),
                   force=[t_m], inc="ve_u")
            op("sync", lambda e: e.dma_start(Wo32[:, :, :], wo_re[:, 4:8, :]),
               reads=[tok("ve_u")], force=[("ld_wo32", 16)], inc="ld_wo32", inc_n=16)
            t_ldwo32b = tok("ld_wo32")
    t_xfm = tok("ve_xfm")
    u_toks = {}

    def emit_uprep_tail():
        for fc in range(4):
            t_m = op("vector", lambda e, fc=fc: e.tensor_mul(xsq[:, :], Wo32[:, fc, :],
                                                             wr_bc[:, :]),
                     reads=[t_ldwo32b] if fc == 0 else (),
                     force=[tok("ve_u")], inc="ve_u")
            op("vector", lambda e, fc=fc: e.reduce_sum(u_cols[:, 4 + fc:5 + fc],
                                                       xsq[:, :], AX.X),
               force=[t_m], inc="ve_u")
        t_uh = op("vector", lambda e: e.tensor_copy(u_hi[:, :], u_cols[:, :]),
                  force=[tok("ve_u")], inc="ve_u")
        op("vector", lambda e: e.tensor_tensor(u_lo[:, :], u_cols[:, :], u_hi[:, :],
                                               ALU.subtract), force=[t_uh], inc="ve_u")
        u_toks["t_u"] = tok("ve_u")
        for tt in range(NT):
            t_m = op("vector", lambda e, tt=tt: e.tensor_mul(xsq[:, :], x_tm[:, tt, :],
                                                             wr_bc[:, :]),
                     reads=[t_ldx] if tt == 0 else (), force=[tok("ve_u")], inc="ve_u")
            op("vector", lambda e, tt=tt: e.reduce_sum(xw_cols[:, tt:tt + 1], xsq[:, :], AX.X),
               force=[t_m], inc="ve_u")
        u_toks["t_sprep"] = tok("ve_u")


    def early_tail():
        out_r0 = out_ap.rearrange("(a p) d -> p a d", p=128)
        for tt0 in range(NT):
            op("sync", lambda e, tt0=tt0: e.dma_start(out_r0[:, tt0, :], x_tm[:, tt0, :]),
               reads=[t_ldx], inc="st_out", inc_n=16)
        t_st0 = tok("st_out")

        def fin(e, t_st0=t_st0):
            e.wait_ge(ctx.sems["st_out"], t_st0[1])
        ctx.q["sync"].append(fin)
        with nc.Block() as blk:
            @blk.tensor
            def _(eng):
                for f in ctx.q["tensor"]:
                    f(eng)

            @blk.vector
            def _(eng):
                for f in ctx.q["vector"]:
                    f(eng)

            @blk.scalar
            def _(eng):
                for f in ctx.q["scalar"]:
                    f(eng)

            @blk.gpsimd
            def _(eng):
                for f in ctx.q["gpsimd"]:
                    f(eng)

            @blk.sync
            def _(eng):
                for f in ctx.q["sync"]:
                    f(eng)
        return nc, sorted(dumps.keys())

    if stop_after <= 0:
        return early_tail()

    A.release(pre_mark)
    csb_mark = CSB_MARK0
    attn_mark = A.mark()
    Wv2 = [A.alloc(f"Wv2_{i}", [128, NDC, 256], F16) for i in range(2)]
    V2_first = A.alloc("V2_0", [128, NT * 264], F16)
    _clo_view = A.view_at(TMPE_START + 512, [128, 1024], F16)
    C_lo16s = [_clo_view[:, 0:512], _clo_view[:, 512:1024]]
    V2 = [V2_first, A.alloc("V2_1", [128, NT * 264], F16)]
    Wqk = A.alloc("Wqk", [128, NDC, 256], F16)
    Qp = A.alloc("Qp", [128, S], F16)
    Kp = A.alloc("Kp", [128, S], F16)
    P_sb = [A.alloc(f"P{i}", [128, 1024], F16) for i in range(2)]
    ddA = tmp_e[0][0:1, 0:512]
    ddB = tmp_e[0][32:33, 0:512]
    ddR = tmp_e[0][0:33, 0:512]
    rd_rep = A.alloc("rd_rep", [128, 512], F32)
    C32s = [xsq[:, 0:512], xsq[:, 512:1024]]
    DELAY_FOODS = 32

    t_pre = [t_rows, t_bcasts, ("pe0", 11), t_b1sc]

    def wv_dma(g):
        src = I["W_attn"][:, 2048 + 256 * g:2048 + 256 * g + 256].rearrange(
            "(dc p) c -> p dc c", p=128)
        op("gpsimd", lambda e, g=g, src=src: e.dma_start(Wv2[g % 2][:, :, :], src),
           reads=([("pe_v", 16 * (g - 1))] if g >= 2 else t_pre),
           force=[("ld_wv", 16 * g)], inc="ld_wv", inc_n=16)

    wv_dma(0)
    wv_dma(1)
    p_free = [None] * 4
    pending_aw = []
    ev_idx = [0]
    cur_t = [0]

    def emit_aw(item, bank):
        cs_, pc_, Q_, t_c32_ = item
        op("tensor",
           lambda e, bank=bank, pc_=pc_, Q_=Q_:
           e.matmul(bank.t[0:1, :], u_hi[:, pc_:pc_ + 1],
                    C_sb[:, pc_, 512 * Q_:512 * Q_ + 512], start=True, stop=False),
           reads=[t_c32_, u_toks["t_u"]] + bank.rw())
        op("tensor",
           lambda e, bank=bank, cs_=cs_, pc_=pc_:
           e.matmul(bank.t[0:1, :], u_hi[:, pc_:pc_ + 1], C_lo16s[cs_][:, :],
                    start=False, stop=False))
        bank.w = op("tensor",
                    lambda e, bank=bank, pc_=pc_, Q_=Q_:
                    e.matmul(bank.t[0:1, :], u_lo[:, pc_:pc_ + 1],
                             C_sb[:, pc_, 512 * Q_:512 * Q_ + 512],
                             start=False, stop=True),
                    inc="pe_aw")
        bank.r = op("vector",
                    lambda e, bank=bank, Q_=Q_:
                    e.tensor_tensor(aw_row[:, 512 * Q_:512 * Q_ + 512], bank.t[0:1, :],
                                    aw_row[:, 512 * Q_:512 * Q_ + 512], ALU.add),
                    reads=[bank.w], force=[tok("ve_aw")], inc="ve_aw")

    for g in range(4):
        buf = g % 2
        t_vones = op("vector", lambda e, buf=buf: e.memset(V2[buf][:, :], 1.0),
                     reads=([("pe_av", 160 * (g - 1))] if (g >= 2 and stop_after >= 2)
                            else (t_pre + ([u_toks["t_u"]] if g == 1 else []))),
                     inc="ve_v")
        for ki in range(NT):
            b = banks[ki % 4]
            for dc in range(NDC):
                w = op("tensor",
                       lambda e, b=b, dc=dc, ki=ki, buf=buf:
                       e.matmul(b.t[:, 0:256], x_fm[:, dc, 128 * ki:128 * ki + 128],
                                Wv2[buf][:, dc, :],
                                start=(dc == 0), stop=(dc == NDC - 1)),
                       reads=([("ve_xfm", 8 * (ki // 4 + 1)), ("ld_wv", 16 * (g + 1))]
                              + b.rw()) if dc == 0 else (),
                       inc="pe_v" if dc == NDC - 1 else None)
            b.w = w
            for p2 in range(2):
                for h in range(2):
                    base = ki * 264 + 132 * p2 + 66 * h
                    fcol = 256 * g + 128 * p2 + 64 * h
                    rt = op("vector",
                            lambda e, b=b, p2=p2, h=h, base=base, fcol=fcol, buf=buf:
                            e.tensor_tensor(V2[buf][:, base:base + 64],
                                            b.t[:, 128 * p2 + 64 * h:128 * p2 + 64 * h + 64],
                                            bv_bc[:, fcol:fcol + 64], ALU.add),
                            reads=[b.w] if (p2 == 0 and h == 0) else (),
                            force=[t_vones, t_bcasts]
                                  if (ki == 0 and p2 == 0 and h == 0) else (),
                            inc="ve_v" if (p2 == 1 and h == 1) else None)
            b.r = rt
            feed(NDC)
        t_vg = tok("ve_v")
        if g == 0:
            emit_uprep_tail()

        if stop_after < 1.3:
            continue
        for pc in (2 * g, 2 * g + 1):
            qr = I["W_attn"][:, 128 * pc:128 * pc + 128].rearrange("(dc p) c -> p dc c", p=128)
            kr = I["W_attn"][:, 1024 + 128 * pc:1024 + 128 * pc + 128].rearrange(
                "(dc p) c -> p dc c", p=128)
            op("gpsimd", lambda e, qr=qr: e.dma_start(Wqk[:, :, 0:128], qr),
               reads=([("pe_qk", 8 * pc)] if pc >= 1 else (t_pre + [u_toks["t_u"]])),
               force=[("ld_qk", 32 * pc)], inc="ld_qk", inc_n=16)
            op("gpsimd", lambda e, kr=kr: e.dma_start(Wqk[:, :, 128:256], kr),
               force=[("ld_qk", 32 * pc + 16)], inc="ld_qk", inc_n=16)
            t_wqk = ("ld_qk", 32 * (pc + 1))
            if pc == 2 * g and g + 2 < 4:
                wv_dma(g + 2)

            qk_toks = {}
            for which, dest, bci in (("q", Qp, pc), ("k", Kp, 8 + pc)):
                coff = 0 if which == "q" else 128
                for dc in range(NDC):
                    for tt in range(NQ):
                        b = banks[tt]
                        w = op("tensor",
                               lambda e, b=b, dc=dc, tt=tt, coff=coff:
                               e.matmul(b.t[:, :], Wqk[:, dc, coff:coff + 128],
                                        x_fm[:, dc, 512 * tt:512 * tt + 512],
                                        start=(dc == 0), stop=(dc == NDC - 1)),
                               reads=([t_wqk, ("ve_xfm", 8 * (tt + 1))] + b.rw())
                                     if dc == 0 else (),
                               inc="pe_qk" if dc == NDC - 1 else None)
                        if dc == NDC - 1:
                            b.w = w
                for tt in range(NQ):
                    b = banks[tt]
                    extra = [("pe_s", 80 * pc)] if (pc >= 1 and stop_after >= 1.6) else []
                    b.r = op("vector",
                             lambda e, b=b, tt=tt, dest=dest, bci=bci:
                             e.tensor_scalar(dest[:, 512 * tt:512 * tt + 512], b.t[:, :],
                                             bqk_cols[:, bci:bci + 1], None, ALU.add),
                             reads=[b.w] + extra, inc="ve_qk")
                    qk_toks[(which, tt)] = b.r
                feed(NDC * NQ)
            t_qk = tok("ve_qk")
            if pc == 0:
                d_q0 = mkdump("q0", [128, S])
                if d_q0 is not None:
                    op("gpsimd", lambda e, d_q0=d_q0: e.dma_start(d_q0, Qp[:, :]),
                       reads=[t_qk], inc="st_dbg", inc_n=16)
                d_k0 = mkdump("k0", [128, S])
                if d_k0 is not None:
                    op("gpsimd", lambda e, d_k0=d_k0: e.dma_start(d_k0, Kp[:, :]),
                       reads=[t_qk], inc="st_dbg", inc_n=16)
                d_v0 = mkdump("v0", [128, NT * 264])
                if d_v0 is not None:
                    op("gpsimd", lambda e, d_v0=d_v0: e.dma_start(d_v0, V2[0][:, :]),
                       reads=[t_vg], inc="st_dbg", inc_n=16)
            if stop_after < 1.6:
                continue

            tiles = []
            for Q in range(NQ):
                nki = 4 * Q + 4
                for ki in range(nki):
                    for head in range(2):
                        tiles.append((Q, ki, head, nki))
            NTIL = len(tiles)
            L = 2
            p_rdy = [None] * NTIL

            def emit_evac(pc, Q, CA, CB, delay=DELAY_FOODS):
                e_i = ev_idx[0]
                ev_idx[0] += 1
                op("vector", lambda e, CA=CA: e.tensor_copy(ddA[:, :], CA.t[64:65, :]),
                   reads=[CA.w, ("pe_rd", 2 * e_i), t_consts], inc="ve_dd")
                t_db = op("vector", lambda e, CB=CB: e.tensor_copy(ddB[:, :], CB.t[64:65, :]),
                          reads=[CB.w], force=[tok("ve_dd")], inc="ve_dd")
                t_r = op("vector", lambda e: e.reciprocal(ddR[:, :], ddR[:, :]),
                         force=[t_db], inc="ve_rd")

                def batch(pc=pc, Q=Q, CA=CA, CB=CB, t_r=t_r, e_i=e_i):
                    rb = banks[(cur_t[0] + 2) % 4]
                    rb.w = op("tensor",
                              lambda e, rb=rb: e.matmul(rb.t[:, :], ones33AB[:, :],
                                                        ddR[:, :], start=True, stop=True),
                              reads=[t_r] + rb.rw(), inc="pe_rd", inc_n=2)
                    t_rep = op("vector", lambda e, rb=rb: e.tensor_copy(rd_rep[:, :], rb.t[:, :]),
                               reads=[rb.w], inc="ve_rd")
                    rb.r = t_rep
                    cs = e_i % 2
                    war_aw = [("pe_aw", e_i - 1)] if e_i >= 2 else []
                    op("vector",
                       lambda e, CA=CA, cs=cs:
                       e.tensor_tensor(C32s[cs][0:64, :], CA.t[0:64, :], rd_rep[0:64, :],
                                       ALU.mult),
                       force=[t_rep], reads=war_aw, inc="ve_c")
                    t_c32 = op("vector",
                               lambda e, CB=CB, cs=cs:
                               e.tensor_tensor(C32s[cs][64:128, :], CB.t[0:64, :],
                                               rd_rep[64:128, :], ALU.mult),
                               inc="ve_c")
                    CA.r = t_c32
                    CB.r = t_c32
                    op("vector",
                       lambda e, cs=cs, pc=pc, Q=Q:
                       e.tensor_copy(C_sb[0:64, pc, 512 * Q:512 * Q + 512], C32s[cs][0:64, :]),
                       force=[t_c32], inc="ve_c")
                    t_cpb = op("vector",
                               lambda e, cs=cs, pc=pc, Q=Q:
                               e.tensor_copy(C_sb[64:128, pc, 512 * Q:512 * Q + 512],
                                             C32s[cs][64:128, :]),
                               inc="ve_c")
                    t_clo = op("vector",
                               lambda e, cs=cs, pc=pc, Q=Q:
                               e.tensor_tensor(C_lo16s[cs][:, :], C32s[cs][:, :],
                                               C_sb[:, pc, 512 * Q:512 * Q + 512],
                                               ALU.subtract),
                               force=[t_cpb], inc="ve_c")
                    if pending_aw:
                        emit_aw(pending_aw.pop(0), rb)
                    pending_aw.append((cs, pc, Q, t_clo))
                defer(delay, batch)

            n_nt = NTIL // 2
            for i_nt in range(n_nt + 1):
                if i_nt < n_nt:
                    pr = i_nt % 2
                    Q, ki, _h, nki = tiles[2 * i_nt]
                    diag = ki >= 4 * Q
                    js = 128 * (ki - 4 * Q) if diag else 0
                    for t in (2 * i_nt, 2 * i_nt + 1):
                        cur_t[0] = t
                        head = t & 1
                        sbank = banks[t % 4]
                        koff = 64 * head
                        sbank.w = op("tensor",
                                     lambda e, sbank=sbank, ki=ki, Q=Q, koff=koff:
                                     e.matmul(sbank.t[:, :],
                                              Kp[koff:koff + 64, 128 * ki:128 * ki + 128],
                                              Qp[koff:koff + 64, 512 * Q:512 * Q + 512],
                                              start=True, stop=True, tile_position=(koff, 0)),
                                     reads=[qk_toks[("q", Q)], qk_toks[("k", ki // 4)]]
                                           + sbank.rw(), inc="pe_s")
                        feed()
                    sw0 = banks[2 * pr].w
                    sw1 = banks[2 * pr + 1].w
                    pf = p_free[pr]
                    if not diag:
                        t_exp = op("scalar",
                                   lambda e, pr=pr:
                                   e.activation(P_sb[pr][:, :], dpair[pr][:, :],
                                                AF.Exp, bias=shift_col[:, 0:1], scale=0.125),
                                   reads=[sw0, sw1, pf, t_consts], inc="ae_exp")
                        banks[2 * pr].r = t_exp
                        banks[2 * pr + 1].r = t_exp
                        p_rdy[2 * i_nt] = t_exp
                        p_rdy[2 * i_nt + 1] = t_exp
                    else:
                        for head in range(2):
                            t_exp = op("scalar",
                                       lambda e, pr=pr, js=js, head=head:
                                       e.activation(P_sb[pr][:, 512 * head + js:512 * head + 512],
                                                    dpair[pr][:, 512 * head + js:512 * head + 512],
                                                    AF.Exp, bias=shift_col[:, 0:1], scale=0.125),
                                       reads=[sw0, sw1, pf, t_consts], inc="ae_exp")
                            banks[2 * pr + head].r = t_exp
                            p_rdy[2 * i_nt + head] = op(
                                "vector",
                                lambda e, pr=pr, js=js, head=head:
                                e.tensor_tensor(P_sb[pr][:, 512 * head + js:512 * head + js + 128],
                                                P_sb[pr][:, 512 * head + js:512 * head + js + 128],
                                                mask_tri[:, :], ALU.mult),
                                reads=[t_exp, t_consts], inc="ge_mask")
                if stop_after < 1.7 or i_nt < 1:
                    continue
                for j in (2 * (i_nt - 1), 2 * (i_nt - 1) + 1):
                    Qj, kij, headj, nkij = tiles[j]
                    CA = banks[4 + 2 * (Qj % 2)]
                    CB = banks[5 + 2 * (Qj % 2)]
                    cb = CA if headj == 0 else CB
                    jsj = 128 * (kij - 4 * Qj) if kij >= 4 * Qj else 0
                    vbase = kij * 264 + 132 * (pc % 2) + 66 * headj
                    reads_av = [p_rdy[j], t_vg]
                    if kij == 0:
                        reads_av += cb.rw()
                    w = op("tensor",
                           lambda e, cb=cb, vbase=vbase, pr=((j // 2) % 2), hh=headj,
                           js=jsj, ki=kij, nki=nkij, buf=buf:
                           e.matmul(cb.t[0:66, js:512], V2[buf][:, vbase:vbase + 66],
                                    P_sb[pr][:, 512 * hh + js:512 * hh + 512],
                                    start=(ki == 0), stop=(ki == nki - 1)),
                           reads=reads_av, inc="pe_av")
                    feed()
                    p_free[(j // 2) % 2] = w
                    if kij == nkij - 1:
                        cb.w = w
                        if headj == 1 and stop_after >= 1.8:
                            emit_evac(pc, Qj, CA, CB,
                                      64 if Qj == NQ - 1 else DELAY_FOODS)
    Wout_bf = A.view_at(attn_mark, [128, NDC * DM], F16).rearrange(
        "p (a b) -> p a b", b=DM)
    wo_r = I["W_out"].rearrange("(fc p) o -> p fc o", p=128)
    op("gpsimd", lambda e: e.dma_start(Wout_bf[:, :, :], wo_r),
       reads=[("pe_v", 64), ("pe_av", 480)], inc="ld_wo", inc_n=16)
    drain_deferred()
    if pending_aw:
        emit_aw(pending_aw.pop(0), banks[(cur_t[0] + 2) % 4])
    t_aw = tok("ve_aw") if "ve_aw" in ctx.counts else None
    t_sprep = u_toks["t_sprep"]

    d_call = mkdump("call", [128, NDC * S])
    if d_call is not None:
        op("gpsimd", lambda e: e.dma_start(d_call.rearrange("p (a b) -> p a b", b=S),
                                           C_sb[:, :, :]),
           reads=[tok("ve_c")], inc="st_dbg", inc_n=16)
    d_c0 = mkdump("c0", [128, S])
    if d_c0 is not None:
        op("gpsimd", lambda e: e.dma_start(d_c0, C_sb[:, 0, :]),
           reads=[("ve_c", 16)], inc="st_dbg", inc_n=16)
    t_attn = [tok("ve_c"), tok("pe_s"), tok("pe_av"), tok("ae_exp"), tok("ge_mask"),
              tok("pe_rd"), tok("ve_rd"), tok("ve_qk"), tok("pe_qk"), tok("pe_v"),
              tok("ve_v"), tok("pe_aw"), tok("ve_aw")]

    if stop_after <= 2:
        return early_tail()

    A.release(attn_mark)
    t_ldwo = tok("ld_wo")

    for tt in range(NT):
        ub = [banks[tt % 2], banks[2 + tt % 2]]
        for half in range(2):
            b = ub[half]
            for fc in range(NDC):
                w = op("tensor",
                       lambda e, b=b, fc=fc, tt=tt, half=half:
                       e.matmul(b.t[:, :], C_sb[:, fc, 128 * tt:128 * tt + 128],
                                Wout_bf[:, fc, 512 * half:512 * half + 512],
                                start=(fc == 0), stop=(fc == NDC - 1)),
                       reads=([t_ldwo] + t_attn + b.rw()) if fc == 0 else (),
                       inc="pe_wo" if fc == NDC - 1 else None)
            b.w = w
        for half in range(2):
            b = ub[half]
            te = tmp_e[half]
            t_te = op("vector",
                      lambda e, b=b, te=te, half=half:
                      e.tensor_tensor(te[:, :], b.t[:, :],
                                      bout_bc[:, 512 * half:512 * half + 512],
                                      ALU.add),
                      reads=[b.w], inc="ve_te")
            b.r = op("vector",
                     lambda e, te=te, tt=tt, half=half:
                     e.tensor_tensor(x_tm[:, tt, 512 * half:512 * half + 512], te[:, :],
                                     x_tm[:, tt, 512 * half:512 * half + 512], ALU.add),
                     force=[t_te], inc="ve_wo")
        op("scalar",
           lambda e, tt=tt: e.activation(xsq[:, :], x_tm[:, tt, :], AF.Square,
                                         accum_out=stats_q[:, tt:tt + 1]),
           reads=[("ve_wo", 2 * (tt + 1))] + ([t_sprep] if tt == 0 else []),
           force=[tok("ae_ln1")], inc="ae_ln1")
        op("vector",
           lambda e, tt=tt: e.reduce_sum(stats_s[:, tt:tt + 1], x_tm[:, tt, :], AX.X),
           force=[("ve_wo", 2 * (tt + 1))], inc="ve_ln1")

    def ln_stats(sem_a, sem_v):
        t1 = op("vector", lambda e: e.tensor_scalar(stats_m[:, :], stats_s[:, :], 1.0 / DM,
                                                    None, ALU.mult),
                reads=[(sem_a, 16)], force=[(sem_v, 16)], inc=sem_v)
        t2 = op("vector", lambda e: e.tensor_tensor(stats_t[:, :], stats_m[:, :], stats_m[:, :],
                                                    ALU.mult),
                force=[t1], inc=sem_v)
        t3 = op("vector", lambda e: e.scalar_tensor_tensor(stats_r[:, :], stats_q[:, :], 1.0 / DM,
                                                           stats_t[:, :], ALU.mult, ALU.subtract),
                force=[t2], inc=sem_v)
        t4 = op("scalar", lambda e: e.activation(stats_t[:, :], stats_r[:, :], AF.Sqrt,
                                                 bias=eps_col[:, 0:1], scale=1.0),
                force=[t3], inc=sem_a)
        t5 = op("vector", lambda e: e.reciprocal(stats_r[:, :], stats_t[:, :]),
                force=[t4], inc=sem_v)
        return t5

    t_rstd1 = ln_stats("ae_ln1", "ve_ln1")
    d_xres = mkdump("xres", [S, DM])
    t_xres_dump = None
    if d_xres is not None:
        drx = d_xres.rearrange("(a p) d -> p a d", p=128)
        for g in range(4):
            t_xres_dump = op("sync", lambda e, g=g, drx=drx: e.dma_start(
                drx[:, 4 * g:4 * g + 4, :], x_tm[:, 4 * g:4 * g + 4, :]),
                reads=[("ve_wo", 32)], inc="st_dbg", inc_n=16)
    t_r2 = op("vector", lambda e: e.tensor_scalar(stats_t[:, :], stats_r[:, :], 2.0,
                                                  None, ALU.mult),
              force=[t_rstd1], inc="ve_x1")
    for tt in range(NT):
        op("vector",
           lambda e, tt=tt: e.tensor_scalar(x_tm[:, tt, :], x_tm[:, tt, :],
                                            stats_m[:, tt:tt + 1], stats_t[:, tt:tt + 1],
                                            ALU.subtract, ALU.mult),
           force=[t_r2] if tt == 0 else (),
           reads=[t_xres_dump] if (tt == 0 and t_xres_dump is not None) else (),
           inc="ve_x1")
    t_x1 = tok("ve_x1")

    d_x1 = mkdump("x1", [S, DM])
    if d_x1 is not None:
        dr = d_x1.rearrange("(a p) d -> p a d", p=128)
        for g in range(4):
            op("sync", lambda e, g=g, dr=dr: e.dma_start(dr[:, 4 * g:4 * g + 4, :],
                                                         x_tm[:, 4 * g:4 * g + 4, :]),
               reads=[t_x1], inc="st_dbg", inc_n=16)
    t_x1dump = tok("st_dbg") if "x1" in debug else None

    if stop_after <= 3:
        return early_tail()

    d_aw = mkdump("aw", [1, S])
    if d_aw is not None:
        op("sync", lambda e: e.dma_start(d_aw, aw_row[:, :]), reads=[t_aw],
           inc="st_dbg", inc_n=16)
    d_xw = mkdump("xw", [128, 16])
    if d_xw is not None:
        op("sync", lambda e: e.dma_start(d_xw, xw_cols[:, :]), reads=[t_sprep],
           inc="st_dbg", inc_n=16)
    b3 = banks[3]
    for k in range(16):
        w = op("tensor",
               lambda e, b3=b3, k=k:
               e.transpose(b3.t[:, k:k + 1], aw_row[0:1, 128 * k:128 * k + 128],
                           ident_f[0:1, 0:1]),
               reads=([t_aw] + b3.rw()) if k == 0 else (),
               inc="pe_rt" if k == 15 else None)
        if k == 15:
            b3.w = w
    b3.r = op("vector", lambda e: e.tensor_copy(aw_cols[:, :], b3.t[:, 0:16]),
              reads=[b3.w], inc="ve_rt")
    s1 = op("vector", lambda e: e.tensor_add(s_cols[:, :], xw_cols[:, :], aw_cols[:, :]),
            force=[b3.r, t_rstd1], reads=[t_sprep], inc="ve_rt")
    s2 = op("vector", lambda e: e.tensor_scalar(s_cols[:, :], s_cols[:, :], bw_sw[:, 0:1],
                                                None, ALU.add),
            force=[s1], inc="ve_rt")
    s3 = op("vector", lambda e: e.scalar_tensor_tensor(s_cols[:, :], stats_m[:, :],
                                                       nsw_col[:, 0:1], s_cols[:, :],
                                                       ALU.mult, ALU.add),
            force=[s2], inc="ve_rt")
    t_scols = op("vector", lambda e: e.tensor_tensor(s_cols[:, :], s_cols[:, :],
                                                     stats_r[:, :], ALU.mult),
                 force=[s3], inc="ve_rt")

    A.release(csb_mark)
    Wq1 = [A.alloc(f"Wq1_{i}", [128, NDC, 1024], F16) for i in range(2)]
    Wq2 = [A.alloc(f"Wq2_{i}", [128, 8, 1024], F16) for i in range(2)]

    def wq_load(q):
        if q == 0:
            reads = [tok("pe_wo"), tok("ve_wo"), ("pe_rt", 1)]
        elif q < 2:
            reads = []
        else:
            reads = [("pe_u", 32 * (q - 1))]
        s1r = I["W1"][:, 1024 * q:1024 * q + 1024].rearrange("(dc p) c -> p dc c", p=128)
        s2r = I["W2"][1024 * q:1024 * q + 1024, :].rearrange("(a p) c -> p a c", p=128)
        op("gpsimd", lambda e, q=q, s1r=s1r: e.dma_start(Wq1[q % 2][:, :, :], s1r),
           reads=reads, force=[("ld_q", 32 * q)], inc="ld_q", inc_n=16)
        op("gpsimd", lambda e, q=q, s2r=s2r: e.dma_start(Wq2[q % 2][:, :, :], s2r),
           force=[("ld_q", 32 * q + 16)], inc="ld_q", inc_n=16)

    wq_load(0)
    wq_load(1)
    for dc in range(NDC):
        for g in range(4):
            b = banks[(dc * 4 + g) % 4]
            for k in range(4):
                ttt = 4 * g + k
                w = op("tensor",
                       lambda e, b=b, k=k, ttt=ttt, dc=dc:
                       e.transpose(b.t[:, 128 * k:128 * k + 128],
                                   x_tm[:, ttt, 128 * dc:128 * dc + 128], ident_f[:, :]),
                       reads=([t_x1] + b.rw()) if k == 0 else (),
                       inc="pe_x1t" if k == 3 else None)
                if k == 3:
                    b.w = w
            b.r = op("vector",
                     lambda e, b=b, dc=dc, g=g:
                     e.tensor_copy(x_fm[:, dc, 512 * g:512 * g + 512], b.t[:, :]),
                     reads=[b.w], inc="ve_x1t")
    t_x1fm = tok("ve_x1t")
    t_x1t_pe = tok("pe_x1t")

    rank_mark = A.mark()
    sT_sb = A.alloc("sT_sb", [16, 128], F32)
    s_bc = A.alloc("s_bc", [128, S], F32)
    s_row = A.alloc("s_row", [1, S], F32)
    cmp0 = A.view_at(TMPE_START, [128, S], F16)
    cmp = [cmp0, cmp0]
    rank_row = s_bc[0:1, 0:S]

    b0 = banks[0]
    b0.w = op("tensor", lambda e: e.transpose(b0.t[0:16, 0:128], s_cols[:, :], ident_f[:, :]),
              reads=[t_scols] + b0.rw(), inc="pe_rk")
    b0.r = op("vector", lambda e: e.tensor_copy(sT_sb[:, :], b0.t[0:16, 0:128]),
              reads=[b0.w], inc="ve_rk")
    t_srow = op("sync", lambda e: e.dma_start(
        s_row[0:1, 0:S].rearrange("p (a b) -> p a b", b=128), sT_sb[:, :]),
        reads=[b0.r, tok("pe_rt")], inc="ld_srow", inc_n=16)
    for j in range(NQ):
        b = banks[j]
        b.w = op("tensor",
                 lambda e, b=b, j=j: e.matmul(b.t[:, :], ones1f[0:1, :],
                                              s_row[0:1, 512 * j:512 * j + 512],
                                              start=True, stop=True),
                 reads=[t_srow] + b.rw(), inc="pe_rk")
        b.r = op("vector",
                 lambda e, b=b, j=j: e.tensor_copy(s_bc[:, 512 * j:512 * j + 512], b.t[:, :]),
                 reads=[b.w], inc="ve_rk")
    t_sbc = tok("ve_rk")

    for k in range(16):
        reads = [("pe_rka", 4 * k)] if k >= 1 else []
        op("vector",
           lambda e, k=k: e.tensor_scalar(cmp[k % 2][:, :], s_bc[:, :],
                                          s_cols[:, k:k + 1], None, ALU.is_lt),
           reads=reads, force=[t_sbc] if k == 0 else (), inc="ve_cmp")
        for j in range(NQ):
            b = banks[4 + j]
            w = op("tensor",
                   lambda e, b=b, k=k, j=j:
                   e.matmul(b.t[0:1, :], onescol_bf[:, 0:1],
                            cmp[k % 2][:, 512 * j:512 * j + 512],
                            start=(k == 0), stop=(k == 15)),
                   reads=[("ve_cmp", k + 1)] + (b.rw() if k == 0 else []),
                   inc="pe_rka")
            if k == 15:
                b.w = w
    for j in range(NQ):
        b = banks[4 + j]
        b.r = op("vector",
                 lambda e, b=b, j=j: e.tensor_copy(rank_row[:, 512 * j:512 * j + 512],
                                                   b.t[0:1, :]),
                 reads=[b.w], inc="ve_rk")
    t_rrow = tok("ve_rk")
    b1b = banks[1]
    for k in range(16):
        w = op("tensor",
               lambda e, b1b=b1b, k=k:
               e.transpose(b1b.t[:, k:k + 1], rank_row[0:1, 128 * k:128 * k + 128],
                           ident_f[0:1, 0:1]),
               reads=([t_rrow] + b1b.rw()) if k == 0 else (),
               inc="pe_rk" if k == 15 else None)
        if k == 15:
            b1b.w = w
    t_flag = op("vector",
                lambda e: e.tensor_scalar(flag_cols[:, :], b1b.t[:, 0:16], 1023.5, None,
                                          ALU.is_lt),
                reads=[b1b.w], inc="ve_rk")
    b1b.r = t_flag
    t_sig = op("scalar", lambda e: e.activation(sig_cols[:, :], s_cols[:, :], AF.Sigmoid),
               reads=[t_scols], inc="ae_rk")
    t_gate = op("vector", lambda e: e.tensor_tensor(gate_cols[:, :], flag_cols[:, :],
                                                    sig_cols[:, :], ALU.mult),
                force=[t_flag, t_sig], inc="ve_rk")

    d_s = mkdump("s", [128, 16])
    if d_s is not None:
        op("sync", lambda e: e.dma_start(d_s, s_cols[:, :]), reads=[t_scols],
           inc="st_dbg", inc_n=16)
    d_gate = mkdump("gate", [128, 16])
    if d_gate is not None:
        op("sync", lambda e: e.dma_start(d_gate, gate_cols[:, :]), reads=[t_gate],
           inc="st_dbg", inc_n=16)

    if stop_after <= 4:
        return early_tail()

    A.release(rank_mark)
    t_rank_readers = [tok("pe_rka"), tok("pe_rk"), tok("ve_rk"), tok("ve_cmp")]
    h_sb = A.alloc("h_sb", [128, 2, 8 * 512], F16)

    W2BANKS = {0: (2, 3), 1: (4, 5), 2: (6, 7), 3: (2, 3)}

    def emit_hstep(step, f8):
        q, tg = step // 4, step % 4
        gfc = 8 * q + f8
        idx = 8 * step + f8
        hb = banks[idx % 2]
        for dc in range(NDC):
            w = op("tensor",
                   lambda e, hb=hb, dc=dc, tg=tg, q=q, f8=f8:
                   e.matmul(hb.t[:, :], Wq1[q % 2][:, dc, 128 * f8:128 * f8 + 128],
                            x_fm[:, dc, 512 * tg:512 * tg + 512],
                            start=(dc == 0), stop=(dc == NDC - 1)),
                   reads=([("ld_q", 32 * (q + 1)), t_x1fm] + hb.rw())
                         if dc == 0 else (),
                   inc="pe_h" if dc == NDC - 1 else None)
        hb.w = w
        hreads = [hb.w]
        if idx == 0:
            hreads += t_rank_readers
        if step >= 2:
            hreads.append(("pe_u", 8 * (step - 1)))
        hb.r = op("scalar",
                  lambda e, hb=hb, step=step, f8=f8, gfc=gfc:
                  e.activation(h_sb[:, step % 2, 512 * f8:512 * f8 + 512], hb.t[:, :],
                               AF.Gelu_apprx_tanh,
                               bias=b1_cols[:, gfc:gfc + 1], scale=0.5),
                  reads=hreads, inc="ge_h")

    def emit_w2group(step, TT):
        q = step // 4
        bpair = [banks[W2BANKS[TT][0]], banks[W2BANKS[TT][1]]]
        for f8 in range(8):
            for half in range(2):
                b = bpair[half]
                reads = []
                if f8 == 0:
                    reads = [("ge_h", 8 * (step + 1)), ("ld_q", 32 * (q + 1))] + b.rw()
                w = op("tensor",
                       lambda e, b=b, f8=f8, TT=TT, half=half, q=q, step=step:
                       e.matmul(b.t[:, :],
                                h_sb[:, step % 2, 512 * f8 + 128 * TT:512 * f8 + 128 * TT + 128],
                                Wq2[q % 2][:, f8, 512 * half:512 * half + 512],
                                start=(f8 == 0), stop=(f8 == 7)),
                       reads=reads, inc="pe_u" if f8 == 7 else None)
                if f8 == 7:
                    b.w = w

    def emit_ffn_evac(step, TT, first):
        q, tg = step // 4, step % 4
        ttg = 4 * tg + TT
        bpair = [banks[W2BANKS[TT][0]], banks[W2BANKS[TT][1]]]
        for half in range(2):
            b = bpair[half]
            gc = gate_cols[:, ttg:ttg + 1]
            xs = (ttg, 512 * half)
            rd = [b.w]
            fo = []
            if first and half == 0:
                rd += [t_x1t_pe]
                fo += [t_gate, t_scols, t_x1]
                if t_x1dump is not None:
                    rd.append(t_x1dump)
            if q == 0:
                te = tmp_e[half]
                t_te = op("vector",
                          lambda e, b=b, te=te, half=half:
                          e.tensor_tensor(te[:, :], b.t[:, :],
                                          b2_bc[:, 512 * half:512 * half + 512], ALU.add),
                          reads=rd, force=fo, inc="ve_te")
                b.r = op("vector",
                         lambda e, te=te, ttg=ttg, half=half, gc=gc:
                         e.scalar_tensor_tensor(x_tm[:, ttg, 512 * half:512 * half + 512],
                                                te[:, :], gc,
                                                x_tm[:, ttg, 512 * half:512 * half + 512],
                                                ALU.mult, ALU.add),
                         force=[t_te], inc="ve_y")
            else:
                b.r = op("vector",
                         lambda e, b=b, ttg=ttg, half=half, gc=gc:
                         e.scalar_tensor_tensor(x_tm[:, ttg, 512 * half:512 * half + 512],
                                                b.t[:, :], gc,
                                                x_tm[:, ttg, 512 * half:512 * half + 512],
                                                ALU.mult, ALU.add),
                         reads=rd, force=fo, inc="ve_y")

    first_evac = [True]
    for step in range(17):
        if step < 16:
            q, tg = step // 4, step % 4
            if tg == 0 and q + 2 < 4:
                wq_load(q + 2)
        if step == 0:
            for f8 in range(8):
                emit_hstep(0, f8)
            continue
        pstep = step - 1
        seq = [("w2", 0), ("h", 0), ("h", 1),
               ("w2", 1), ("h", 2), ("h", 3), ("ev", 0),
               ("w2", 2), ("h", 4), ("h", 5), ("ev", 1),
               ("w2", 3), ("h", 6), ("h", 7), ("ev", 2), ("ev", 3)]
        for kind, i in seq:
            if kind == "h":
                if step < 16:
                    emit_hstep(step, i)
            elif kind == "w2":
                emit_w2group(pstep, i)
            else:
                emit_ffn_evac(pstep, i, first_evac[0])
                first_evac[0] = False

    if stop_after <= 5:
        return early_tail()

    d_y = mkdump("y", [S, DM])
    t_ydump = None
    if d_y is not None:
        dry = d_y.rearrange("(a p) d -> p a d", p=128)
        for g in range(4):
            t_ydump = op("sync", lambda e, g=g, dry=dry: e.dma_start(
                dry[:, 4 * g:4 * g + 4, :], x_tm[:, 4 * g:4 * g + 4, :]),
                reads=[("ve_y", 128)], inc="st_dbg", inc_n=16)

    out_r = out_ap.rearrange("(a p) d -> p a d", p=128)
    for gL in range(4):
        sl = slice(4 * gL, 4 * gL + 4)
        for tt in range(4 * gL, 4 * gL + 4):
            vy = 96 + 8 * gL + 2 * (tt % 4) + 2
            op("scalar",
               lambda e, tt=tt: e.activation(xsq[:, :], x_tm[:, tt, :], AF.Square,
                                             accum_out=stats_q[:, tt:tt + 1]),
               reads=[("ve_y", vy)] + ([t_ydump] if t_ydump is not None else [])
                     + ([("ae_ln1", 16)] if (gL == 0 and tt == 0) else []),
               force=[tok("ae_ln2")], inc="ae_ln2")
            op("vector",
               lambda e, tt=tt: e.reduce_sum(stats_s[:, tt:tt + 1], x_tm[:, tt, :], AX.X),
               force=[("ve_y", vy)], inc="ve_ln2")
        t1 = op("vector", lambda e, sl=sl: e.tensor_scalar(stats_m[:, sl], stats_s[:, sl],
                                                           1.0 / DM, None, ALU.mult),
                reads=[("ae_ln2", 4 * (gL + 1))], force=[("ve_ln2", 4 * (gL + 1))],
                inc="ve_ln2x")
        t2 = op("vector", lambda e, sl=sl: e.tensor_tensor(stats_t[:, sl], stats_m[:, sl],
                                                           stats_m[:, sl], ALU.mult),
                force=[t1], inc="ve_ln2x")
        t3 = op("vector", lambda e, sl=sl: e.scalar_tensor_tensor(stats_r[:, sl],
                                                                  stats_q[:, sl], 1.0 / DM,
                                                                  stats_t[:, sl], ALU.mult,
                                                                  ALU.subtract),
                force=[t2], inc="ve_ln2x")
        t4 = op("scalar", lambda e, sl=sl: e.activation(stats_t[:, sl], stats_r[:, sl],
                                                        AF.Sqrt, bias=eps_col[:, 0:1],
                                                        scale=1.0),
                reads=[("ae_ln2", 4 * (gL + 1))], force=[t3], inc="ae_ln2x")
        t5 = op("vector", lambda e, sl=sl: e.reciprocal(stats_r[:, sl], stats_t[:, sl]),
                force=[t4], inc="ve_ln2x")
        for tt in range(4 * gL, 4 * gL + 4):
            t_n = op("vector",
                     lambda e, tt=tt: e.tensor_scalar(x_tm[:, tt, :], x_tm[:, tt, :],
                                                      stats_m[:, tt:tt + 1],
                                                      stats_r[:, tt:tt + 1],
                                                      ALU.subtract, ALU.mult),
                     force=[t5] if tt == 4 * gL else (), inc="ve_out")
            op("sync", lambda e, tt=tt, out_r=out_r: e.dma_start(out_r[:, tt, :],
                                                                 x_tm[:, tt, :]),
               reads=[t_n], inc="st_out", inc_n=16)

    t_st = tok("st_out")
    t_dbg = tok("st_dbg") if "st_dbg" in ctx.counts else None

    def final(e, t_st=t_st, t_dbg=t_dbg):
        e.wait_ge(ctx.sems["st_out"], t_st[1])
        if t_dbg is not None and t_dbg[1] > 0:
            e.wait_ge(ctx.sems["st_dbg"], t_dbg[1])
    ctx.q["sync"].append(final)

    with nc.Block() as blk:
        @blk.tensor
        def _(eng):
            for f in ctx.q["tensor"]:
                f(eng)

        @blk.vector
        def _(eng):
            for f in ctx.q["vector"]:
                f(eng)

        @blk.scalar
        def _(eng):
            for f in ctx.q["scalar"]:
                f(eng)

        @blk.gpsimd
        def _(eng):
            for f in ctx.q["gpsimd"]:
                f(eng)

        @blk.sync
        def _(eng):
            for f in ctx.q["sync"]:
                f(eng)


    return nc, sorted(dumps.keys())


_CACHE = {}


def kernel(L0_attn_encodings, L0_attn_W_attn, L0_attn_b_attn, L0_attn_W_out,
           L0_attn_b_out, L0_ffn_w_router, L0_ffn_W1, L0_ffn_b1, L0_ffn_W2,
           L0_ffn_b2, _debug=()):
    key = tuple(sorted(_debug))
    if key not in _CACHE:
        _CACHE[key] = build(debug=_debug)
    nc, dump_names = _CACHE[key]

    x = np.ascontiguousarray(np.asarray(L0_attn_encodings, dtype=np.float32))
    shared = {
        "W_attn": np.ascontiguousarray(np.asarray(L0_attn_W_attn, np.float32)),
        "b_attn": np.ascontiguousarray(np.asarray(L0_attn_b_attn, np.float32)),
        "W_out": np.ascontiguousarray(np.asarray(L0_attn_W_out, np.float32)),
        "b_out": np.ascontiguousarray(np.asarray(L0_attn_b_out, np.float32)),
        "w_router": np.ascontiguousarray(np.asarray(L0_ffn_w_router, np.float32)),
        "W1": np.ascontiguousarray(np.asarray(L0_ffn_W1, np.float32)),
        "b1": np.ascontiguousarray(np.asarray(L0_ffn_b1, np.float32)),
        "W2": np.ascontiguousarray(np.asarray(L0_ffn_W2, np.float32)),
        "b2": np.ascontiguousarray(np.asarray(L0_ffn_b2, np.float32)),
    }
    in_maps = [dict(shared, x=np.ascontiguousarray(x[b])) for b in range(8)]
    res = run_bass_kernel_spmd(nc, in_maps, core_ids=list(range(8)))
    outs = np.stack([np.asarray(res.results[b]["out"]) for b in range(8)], axis=0)
    if _debug:
        dbg = [{k: np.asarray(res.results[b]["dbg_" + k]) for k in dump_names}
               for b in range(8)]
        return outs, dbg
    return outs


# version 30 (final)
# speedup vs baseline: 1.0430x; 1.0430x over previous
import numpy as np

import concourse.bass as bass
import concourse.mybir as mybir
from concourse.bass_utils import run_bass_kernel_spmd

F32 = mybir.dt.float32
BF16 = mybir.dt.bfloat16
F16 = mybir.dt.float16
AF = mybir.ActivationFunctionType
ALU = mybir.AluOpType
AX = mybir.AxisListType

S = 2048
DM = 1024
NP = 8
NDC = DM // 128
FF = 4096
NFC = FF // 128
NT = S // 128
NQ = S // 512
GELU_C = 0.044715
GELU_SC = float(np.sqrt(GELU_C))
SIG_SCALE = 2.0 * 0.7978845608028654
EXP_SHIFT = -2.0
EPS = 1e-5

ENGINES = ("tensor", "vector", "scalar", "gpsimd", "sync")


class Ctx:
    def __init__(self, nc):
        self.nc = nc
        self.q = {e: [] for e in ENGINES}
        self.counts = {}
        self.sems = {}
        self.satisfied = {}

    def _sem(self, name):
        if name not in self.sems:
            self.sems[name] = self.nc.alloc_semaphore(name)
            self.counts[name] = 0
        return self.sems[name]

    def tok(self, name):
        self._sem(name)
        return (name, self.counts[name])

    def op(self, eng, fn, reads=(), force=(), inc=None, inc_n=1):
        waits = []
        for (sname, val) in tuple(force):
            if val > 0:
                self._sem(sname)
                waits.append((self.sems[sname], val))
                key = (eng, sname)
                if self.satisfied.get(key, 0) < val:
                    self.satisfied[key] = val
        for item in tuple(reads):
            if item is None:
                continue
            sname, val = item
            if val <= 0:
                continue
            self._sem(sname)
            key = (eng, sname)
            if self.satisfied.get(key, 0) >= val:
                continue
            self.satisfied[key] = val
            waits.append((self.sems[sname], val))
        tokv = None
        sh = None
        if inc is not None:
            s = self._sem(inc)
            self.counts[inc] += inc_n
            tokv = (inc, self.counts[inc])
            self.satisfied[(eng, inc)] = self.counts[inc]
            sh = (s, inc_n)

        def run(e, fn=fn, waits=tuple(waits), sh=sh):
            for (s, v) in waits:
                e.wait_ge(s, v)
            inst = fn(e)
            if sh is not None:
                inst.then_inc(sh[0], sh[1])
        self.q[eng].append(run)
        return tokv


class Bank:
    def __init__(self, t):
        self.t = t
        self.w = None
        self.r = None

    def rw(self):
        return [self.r] if self.r is not None else []


class Arena:
    def __init__(self, nc, n_f32):
        self.t = nc.alloc_sbuf_tensor("arena", [128, n_f32], F32)
        self.n = n_f32
        self.ptr = 0

    def alloc(self, name, shape, dtype):
        elems = 1
        for d in shape[1:]:
            elems *= d
        per32 = (elems + 1) // 2 if dtype in (BF16, F16) else elems
        start = (self.ptr + 15) & ~15
        assert start + per32 <= self.n, f"arena OOM at {name}: {start}+{per32}>{self.n}"
        self.ptr = start + per32
        v = self.t[:, :].bitcast(dtype) if dtype != F32 else self.t[:, :]
        off = start * 2 if dtype in (BF16, F16) else start
        flat = v[0:shape[0], off:off + elems]
        if len(shape) == 3:
            flat = flat.rearrange("p (a b) -> p a b", b=shape[2])
        return flat

    def view_at(self, start, shape, dtype):
        elems = 1
        for d in shape[1:]:
            elems *= d
        v = self.t[:, :].bitcast(dtype) if dtype != F32 else self.t[:, :]
        off = start * 2 if dtype in (BF16, F16) else start
        return v[0:shape[0], off:off + elems]

    def mark(self):
        return self.ptr

    def release(self, m):
        self.ptr = m


def build(debug=(), stop_after=9):

    nc = bass.Bass("TRN2", target_bir_lowering=False)
    I = {}
    I["x"] = nc.dram_tensor("x", [S, DM], F32, kind="ExternalInput").ap()
    I["W_attn"] = nc.dram_tensor("W_attn", [DM, 3 * DM], F32, kind="ExternalInput").ap()
    I["b_attn"] = nc.dram_tensor("b_attn", [3 * DM], F32, kind="ExternalInput").ap()
    I["W_out"] = nc.dram_tensor("W_out", [DM, DM], F32, kind="ExternalInput").ap()
    I["b_out"] = nc.dram_tensor("b_out", [DM], F32, kind="ExternalInput").ap()
    I["w_router"] = nc.dram_tensor("w_router", [DM], F32, kind="ExternalInput").ap()
    I["W1"] = nc.dram_tensor("W1", [DM, FF], F32, kind="ExternalInput").ap()
    I["b1"] = nc.dram_tensor("b1", [FF], F32, kind="ExternalInput").ap()
    I["W2"] = nc.dram_tensor("W2", [FF, DM], F32, kind="ExternalInput").ap()
    I["b2"] = nc.dram_tensor("b2", [DM], F32, kind="ExternalInput").ap()
    out_ap = nc.dram_tensor("out", [S, DM], F32, kind="ExternalOutput").ap()

    dumps = {}

    def mkdump(name, shape):
        if name in debug:
            dumps[name] = nc.dram_tensor("dbg_" + name, shape, F32, kind="ExternalOutput").ap()
            return dumps[name]
        return None

    ctx = Ctx(nc)
    op, tok = ctx.op, ctx.tok
    A = Arena(nc, 53184)

    pe_food = [0]
    pe_pending = []

    def feed(n=1):
        pe_food[0] += n
        while pe_pending and pe_pending[0][0] <= pe_food[0]:
            pe_pending.pop(0)[1]()

    def defer(delay, fn):
        pe_pending.append((pe_food[0] + delay, fn))

    def drain_deferred():
        while pe_pending:
            pe_pending.pop(0)[1]()

    dpair = [nc.alloc_psum_tensor(f"dp{i}", [128, 1024], F32) for i in range(2)]
    banks = [Bank(dpair[0][:, 0:512]), Bank(dpair[0][:, 512:1024]),
             Bank(dpair[1][:, 0:512]), Bank(dpair[1][:, 512:1024])]
    banks += [Bank(nc.alloc_psum_tensor(f"pb{bi}", [128, 512], F32)) for bi in range(4, 8)]

    x_tm = A.alloc("x_tm", [128, NT, DM], F32)
    x_fm = A.alloc("x_fm", [128, NDC, S], F16)
    ident_f = A.alloc("ident_f", [128, 128], F32)
    ones1f = A.alloc("ones1f", [1, 128], F32)
    ones33AB = A.alloc("ones33AB", [33, 128], F32)
    onescol_bf = A.alloc("onescol_bf", [128, 16], F16)
    shift_col = A.alloc("shift_col", [128, 1], F32)
    eps_col = A.alloc("eps_col", [128, 1], F32)
    bqk_cols = A.alloc("bqk_cols", [128, 16], F32)
    b1_cols = A.alloc("b1_cols", [128, 32], F32)
    b2_bc = A.alloc("b2_bc", [128, DM], F32)
    wr_bc = A.alloc("wr_bc", [128, DM], F32)
    stats_s = A.alloc("stats_s", [128, 16], F32)
    stats_q = A.alloc("stats_q", [128, 16], F32)
    stats_m = A.alloc("stats_m", [128, 16], F32)
    stats_r = A.alloc("stats_r", [128, 16], F32)
    stats_t = A.alloc("stats_t", [128, 16], F32)
    s_cols = A.alloc("s_cols", [128, 16], F32)
    sig_cols = A.alloc("sig_cols", [128, 16], F32)
    flag_cols = A.alloc("flag_cols", [128, 16], F32)
    gate_cols = A.alloc("gate_cols", [128, 16], F32)
    xsq = A.alloc("xsq", [128, DM], F32)
    mask_tri = A.alloc("mask_tri", [128, 128], F16)
    u_cols = A.alloc("u_cols", [128, NDC], F32)
    u_hi = A.alloc("u_hi", [128, NDC], F16)
    u_lo = A.alloc("u_lo", [128, NDC], F16)
    xw_cols = A.alloc("xw_cols", [128, 16], F32)
    aw_cols = A.alloc("aw_cols", [128, 16], F32)
    bw_sw = A.alloc("bw_sw", [128, 2], F32)
    nsw_col = A.alloc("nsw_col", [128, 1], F32)
    aw_row = A.alloc("aw_row", [1, S], F32)
    bs_row = A.alloc("bs_row", [1, 2], F32)
    CSB_ALLOCS_NOTE = True
    TMPE_START = (A.ptr + 15) & ~15
    tmp_e = [A.alloc(f"tmp_e{i}", [128, 512], F32) for i in range(2)]
    CSB_MARK0 = A.mark()
    C_sb = A.alloc("C_sb", [128, NDC, S], F16)
    bv_bc = A.alloc("bv_bc", [128, DM], F32)
    bout_bc = A.alloc("bout_bc", [128, DM], F32)

    pre_mark = A.mark()
    bqk_pack = A.alloc("bqk_pack", [16, 128], F32)
    b1_pack = A.alloc("b1_pack", [32, 128], F32)
    bv_row = A.alloc("bv_row", [1, DM], F32)
    bout_row = A.alloc("bout_row", [1, DM], F32)
    b2_row = A.alloc("b2_row", [1, DM], F32)
    wr_row = A.alloc("wr_row", [1, DM], F32)
    Wo32 = A.alloc("Wo32", [128, NDC // 2, DM], F32)

    x_r = I["x"].rearrange("(a p) d -> p a d", p=128)
    for dst, src in (
        (bqk_pack, I["b_attn"][0:2048].rearrange("(r j) -> r j", j=128)),
        (bv_row, I["b_attn"][2048:3072].rearrange("(a j) -> a j", a=1)),
        (bout_row, I["b_out"].rearrange("(a j) -> a j", a=1)),
        (b2_row, I["b2"].rearrange("(a j) -> a j", a=1)),
        (wr_row, I["w_router"].rearrange("(a j) -> a j", a=1)),
        (b1_pack, I["b1"].rearrange("(r j) -> r j", j=128)),
    ):
        op("sync", lambda e, dst=dst, src=src: e.dma_start(dst[:, :], src), inc="ld_b", inc_n=16)
    t_ldb = tok("ld_b")
    op("sync", lambda e: e.dma_start(x_tm[:, 0:4, :], x_r[:, 0:4, :]),
       inc="ld_x", inc_n=16)
    _ = None
    wo_re = I["W_out"].rearrange("(fc p) o -> p fc o", p=128)
    op("sync", lambda e: e.dma_start(Wo32[:, :, :], wo_re[:, 0:4, :]),
       inc="ld_wo32", inc_n=16)
    t_ldwo32 = tok("ld_wo32")
    for g in range(1, 4):
        op("sync", lambda e, g=g: e.dma_start(x_tm[:, 4 * g:4 * g + 4, :], x_r[:, 4 * g:4 * g + 4, :]),
           force=[("ld_x", 16 * g)], inc="ld_x", inc_n=16)
    t_ldx = tok("ld_x")

    tg = op("gpsimd", lambda e: e.memset(ident_f[:, :], 1.0), inc="g_const")
    tg = op("gpsimd", lambda e: e.affine_select(ident_f[:, :], ident_f[:, :], [[-1, 128]],
                                                ALU.is_equal, 0.0, base=0,
                                                channel_multiplier=1),
            force=[tg], inc="g_const")
    op("gpsimd", lambda e: e.memset(ones1f[:, :], 1.0))
    tg = op("gpsimd", lambda e: e.memset(ones33AB[:, :], 0.0), inc="g_const")
    tg = op("gpsimd", lambda e: e.memset(ones33AB[0:1, 0:64], 1.0), force=[tg],
            inc="g_const")
    tg = op("gpsimd", lambda e: e.memset(ones33AB[32:33, 64:128], 1.0), force=[tg],
            inc="g_const")
    op("gpsimd", lambda e: e.memset(ddR[:, :], 1.0))
    op("gpsimd", lambda e: e.memset(onescol_bf[:, :], 1.0))
    tg = op("gpsimd", lambda e: e.memset(mask_tri[:, :], 1.0), force=[tg], inc="g_const")
    tg = op("gpsimd", lambda e: e.affine_select(mask_tri[:, :], mask_tri[:, :], [[1, 128]],
                                                ALU.is_ge, 0.0, base=0,
                                                channel_multiplier=-1),
            force=[tg], inc="g_const")
    op("gpsimd", lambda e: e.memset(shift_col[:, :], EXP_SHIFT))
    t_consts = op("gpsimd", lambda e: e.memset(eps_col[:, :], EPS), force=[tg],
                  inc="g_const")

    b4, b5 = banks[4], banks[5]
    b4.w = op("tensor", lambda e: e.transpose(b4.t[:, 0:16], bqk_pack[:, :], ident_f[0:16, 0:16]),
              reads=[t_ldb, t_consts] + b4.rw(), inc="pe0")
    b4.r = op("vector", lambda e: e.tensor_copy(bqk_cols[:, :], b4.t[:, 0:16]),
              reads=[b4.w], inc="ve_b")
    b5.w = op("tensor", lambda e: e.transpose(b5.t[:, 0:32], b1_pack[:, :], ident_f[0:32, 0:32]),
              reads=[t_ldb, t_consts] + b5.rw(), inc="pe0")
    b5.r = op("vector", lambda e: e.tensor_copy(b1_cols[:, :], b5.t[:, 0:32]),
              reads=[b5.w], inc="ve_b")
    t_b1sc = op("vector", lambda e: e.tensor_scalar(stats_t[:, 0:2], b1_cols[:, 0:2],
                                                    GELU_SC, None, ALU.mult),
                force=[b5.r], inc="ve_b")

    for row_ap, dst in ((bv_row, bv_bc), (bout_row, bout_bc), (b2_row, b2_bc), (wr_row, wr_bc)):
        for half in range(2):
            b = banks[6 + half]
            b.w = op("tensor",
                     lambda e, b=b, half=half, row_ap=row_ap:
                     e.matmul(b.t[:, :], ones1f[0:1, :], row_ap[0:1, 512 * half:512 * half + 512],
                              start=True, stop=True),
                     reads=[t_ldb, t_consts] + b.rw(), inc="pe0")
            b.r = op("vector",
                     lambda e, b=b, half=half, dst=dst:
                     e.tensor_copy(dst[:, 512 * half:512 * half + 512], b.t[:, :]),
                     reads=[b.w], inc="ve_b")
    t_bcasts = tok("ve_b")

    t_m0 = op("vector", lambda e: e.tensor_mul(xsq[0:1, 0:DM], bout_row[:, :], wr_row[:, :]),
              reads=[t_ldb], inc="ve_u")
    op("vector", lambda e: e.reduce_sum(bs_row[:, 0:1], xsq[0:1, 0:DM], AX.X),
       force=[t_m0], inc="ve_u")
    t_bs = op("vector", lambda e: e.reduce_sum(bs_row[:, 1:2], wr_row[:, :], AX.X), inc="ve_u")
    b7 = banks[7]
    b7.w = op("tensor", lambda e: e.matmul(b7.t[:, 0:2], ones1f[0:1, :], bs_row[0:1, :],
                                           start=True, stop=True),
              reads=[t_bs] + b7.rw(), inc="pe0")
    b7.r = op("vector", lambda e: e.tensor_copy(bw_sw[:, :], b7.t[:, 0:2]),
              reads=[b7.w], inc="ve_u")
    t_nsw = op("vector", lambda e: e.tensor_scalar(nsw_col[:, :], bw_sw[:, 1:2], -1.0,
                                                   None, ALU.mult),
               force=[b7.r], inc="ve_u")
    op("vector", lambda e: e.memset(aw_row[:, :], 0.0), inc="ve_u")
    t_rows = tok("ve_u")

    for g in range(4):
        for dc in range(NDC):
            b = banks[(g * NDC + dc) % 4]
            for k in range(4):
                ttt = 4 * g + k
                w = op("tensor",
                       lambda e, b=b, k=k, ttt=ttt, dc=dc:
                       e.transpose(b.t[:, 128 * k:128 * k + 128],
                                   x_tm[:, ttt, 128 * dc:128 * dc + 128], ident_f[:, :]),
                       reads=([("ld_x", 16 * (g + 1)), t_consts] + b.rw()) if k == 0 else (),
                       inc="pe0" if k == 3 else None)
                if k == 3:
                    b.w = w
            b.r = op("vector",
                     lambda e, b=b, dc=dc, g=g:
                     e.tensor_copy(x_fm[:, dc, 512 * g:512 * g + 512], b.t[:, :]),
                     reads=[b.w], inc="ve_xfm")
        if g == 0:
            for fc in range(4):
                t_m = op("vector", lambda e, fc=fc: e.tensor_mul(xsq[:, :], Wo32[:, fc, :],
                                                                 wr_bc[:, :]),
                         reads=[t_ldwo32, t_bcasts] if fc == 0 else (),
                         force=[tok("ve_u")] if fc > 0 else (), inc="ve_u")
                op("vector", lambda e, fc=fc: e.reduce_sum(u_cols[:, fc:fc + 1],
                                                           xsq[:, :], AX.X),
                   force=[t_m], inc="ve_u")
            op("sync", lambda e: e.dma_start(Wo32[:, :, :], wo_re[:, 4:8, :]),
               reads=[tok("ve_u")], force=[("ld_wo32", 16)], inc="ld_wo32", inc_n=16)
            t_ldwo32b = tok("ld_wo32")
    t_xfm = tok("ve_xfm")
    u_toks = {}

    def emit_uprep_tail():
        for fc in range(4):
            t_m = op("vector", lambda e, fc=fc: e.tensor_mul(xsq[:, :], Wo32[:, fc, :],
                                                             wr_bc[:, :]),
                     reads=[t_ldwo32b] if fc == 0 else (),
                     force=[tok("ve_u")], inc="ve_u")
            op("vector", lambda e, fc=fc: e.reduce_sum(u_cols[:, 4 + fc:5 + fc],
                                                       xsq[:, :], AX.X),
               force=[t_m], inc="ve_u")
        t_uh = op("vector", lambda e: e.tensor_copy(u_hi[:, :], u_cols[:, :]),
                  force=[tok("ve_u")], inc="ve_u")
        op("vector", lambda e: e.tensor_tensor(u_lo[:, :], u_cols[:, :], u_hi[:, :],
                                               ALU.subtract), force=[t_uh], inc="ve_u")
        u_toks["t_u"] = tok("ve_u")
        for tt in range(NT):
            t_m = op("vector", lambda e, tt=tt: e.tensor_mul(xsq[:, :], x_tm[:, tt, :],
                                                             wr_bc[:, :]),
                     reads=[t_ldx] if tt == 0 else (), force=[tok("ve_u")], inc="ve_u")
            op("vector", lambda e, tt=tt: e.reduce_sum(xw_cols[:, tt:tt + 1], xsq[:, :], AX.X),
               force=[t_m], inc="ve_u")
        u_toks["t_sprep"] = tok("ve_u")


    def early_tail():
        out_r0 = out_ap.rearrange("(a p) d -> p a d", p=128)
        for tt0 in range(NT):
            op("sync", lambda e, tt0=tt0: e.dma_start(out_r0[:, tt0, :], x_tm[:, tt0, :]),
               reads=[t_ldx], inc="st_out", inc_n=16)
        t_st0 = tok("st_out")

        def fin(e, t_st0=t_st0):
            e.wait_ge(ctx.sems["st_out"], t_st0[1])
        ctx.q["sync"].append(fin)
        with nc.Block() as blk:
            @blk.tensor
            def _(eng):
                for f in ctx.q["tensor"]:
                    f(eng)

            @blk.vector
            def _(eng):
                for f in ctx.q["vector"]:
                    f(eng)

            @blk.scalar
            def _(eng):
                for f in ctx.q["scalar"]:
                    f(eng)

            @blk.gpsimd
            def _(eng):
                for f in ctx.q["gpsimd"]:
                    f(eng)

            @blk.sync
            def _(eng):
                for f in ctx.q["sync"]:
                    f(eng)
        return nc, sorted(dumps.keys())

    if stop_after <= 0:
        return early_tail()

    A.release(pre_mark)
    csb_mark = CSB_MARK0
    attn_mark = A.mark()
    Wv2 = [A.alloc(f"Wv2_{i}", [128, NDC, 256], F16) for i in range(2)]
    V2_first = A.alloc("V2_0", [128, NT * 264], F16)
    _clo_view = A.view_at(TMPE_START + 512, [128, 1024], F16)
    C_lo16s = [_clo_view[:, 0:512], _clo_view[:, 512:1024]]
    V2 = [V2_first, A.alloc("V2_1", [128, NT * 264], F16)]
    Wqk = A.alloc("Wqk", [128, NDC, 256], F16)
    Qp = A.alloc("Qp", [128, S], F16)
    Kp = A.alloc("Kp", [128, S], F16)
    P_sb = [A.alloc(f"P{i}", [128, 1024], F16) for i in range(3)]
    ddA = tmp_e[0][0:1, 0:512]
    ddB = tmp_e[0][32:33, 0:512]
    ddR = tmp_e[0][0:33, 0:512]
    rd_rep = A.alloc("rd_rep", [128, 512], F32)
    C32s = [xsq[:, 0:512], xsq[:, 512:1024]]
    DELAY_FOODS = 32

    t_pre = [t_rows, t_bcasts, ("pe0", 11), t_b1sc]

    def wv_dma(g):
        src = I["W_attn"][:, 2048 + 256 * g:2048 + 256 * g + 256].rearrange(
            "(dc p) c -> p dc c", p=128)
        op("gpsimd", lambda e, g=g, src=src: e.dma_start(Wv2[g % 2][:, :, :], src),
           reads=([("pe_v", 16 * (g - 1))] if g >= 2 else t_pre),
           force=[("ld_wv", 16 * g)], inc="ld_wv", inc_n=16)

    wv_dma(0)
    wv_dma(1)
    p_free = [None] * 4
    pending_aw = []
    ev_idx = [0]
    cur_t = [0]

    def emit_aw(item, bank):
        cs_, pc_, Q_, t_c32_ = item
        op("tensor",
           lambda e, bank=bank, pc_=pc_, Q_=Q_:
           e.matmul(bank.t[0:1, :], u_hi[:, pc_:pc_ + 1],
                    C_sb[:, pc_, 512 * Q_:512 * Q_ + 512], start=True, stop=False),
           reads=[t_c32_, u_toks["t_u"]] + bank.rw())
        op("tensor",
           lambda e, bank=bank, cs_=cs_, pc_=pc_:
           e.matmul(bank.t[0:1, :], u_hi[:, pc_:pc_ + 1], C_lo16s[cs_][:, :],
                    start=False, stop=False))
        bank.w = op("tensor",
                    lambda e, bank=bank, pc_=pc_, Q_=Q_:
                    e.matmul(bank.t[0:1, :], u_lo[:, pc_:pc_ + 1],
                             C_sb[:, pc_, 512 * Q_:512 * Q_ + 512],
                             start=False, stop=True),
                    inc="pe_aw")
        bank.r = op("vector",
                    lambda e, bank=bank, Q_=Q_:
                    e.tensor_tensor(aw_row[:, 512 * Q_:512 * Q_ + 512], bank.t[0:1, :],
                                    aw_row[:, 512 * Q_:512 * Q_ + 512], ALU.add),
                    reads=[bank.w], force=[tok("ve_aw")], inc="ve_aw")

    for g in range(4):
        buf = g % 2
        t_vones = op("vector", lambda e, buf=buf: e.memset(V2[buf][:, :], 1.0),
                     reads=([("pe_av", 160 * (g - 1))] if (g >= 2 and stop_after >= 2)
                            else (t_pre + ([u_toks["t_u"]] if g == 1 else []))),
                     inc="ve_v")
        for ki in range(NT):
            b = banks[ki % 4]
            for dc in range(NDC):
                w = op("tensor",
                       lambda e, b=b, dc=dc, ki=ki, buf=buf:
                       e.matmul(b.t[:, 0:256], x_fm[:, dc, 128 * ki:128 * ki + 128],
                                Wv2[buf][:, dc, :],
                                start=(dc == 0), stop=(dc == NDC - 1)),
                       reads=([("ve_xfm", 8 * (ki // 4 + 1)), ("ld_wv", 16 * (g + 1))]
                              + b.rw()) if dc == 0 else (),
                       inc="pe_v" if dc == NDC - 1 else None)
            b.w = w
            for p2 in range(2):
                for h in range(2):
                    base = ki * 264 + 132 * p2 + 66 * h
                    fcol = 256 * g + 128 * p2 + 64 * h
                    rt = op("vector",
                            lambda e, b=b, p2=p2, h=h, base=base, fcol=fcol, buf=buf:
                            e.tensor_tensor(V2[buf][:, base:base + 64],
                                            b.t[:, 128 * p2 + 64 * h:128 * p2 + 64 * h + 64],
                                            bv_bc[:, fcol:fcol + 64], ALU.add),
                            reads=[b.w] if (p2 == 0 and h == 0) else (),
                            force=[t_vones, t_bcasts]
                                  if (ki == 0 and p2 == 0 and h == 0) else (),
                            inc="ve_v" if (p2 == 1 and h == 1) else None)
            b.r = rt
            feed(NDC)
        t_vg = tok("ve_v")
        if g == 0:
            emit_uprep_tail()

        if stop_after < 1.3:
            continue
        for pc in (2 * g, 2 * g + 1):
            qr = I["W_attn"][:, 128 * pc:128 * pc + 128].rearrange("(dc p) c -> p dc c", p=128)
            kr = I["W_attn"][:, 1024 + 128 * pc:1024 + 128 * pc + 128].rearrange(
                "(dc p) c -> p dc c", p=128)
            op("gpsimd", lambda e, qr=qr: e.dma_start(Wqk[:, :, 0:128], qr),
               reads=([("pe_qk", 8 * pc)] if pc >= 1 else (t_pre + [u_toks["t_u"]])),
               force=[("ld_qk", 32 * pc)], inc="ld_qk", inc_n=16)
            op("gpsimd", lambda e, kr=kr: e.dma_start(Wqk[:, :, 128:256], kr),
               force=[("ld_qk", 32 * pc + 16)], inc="ld_qk", inc_n=16)
            t_wqk = ("ld_qk", 32 * (pc + 1))
            if pc == 2 * g and g + 2 < 4:
                wv_dma(g + 2)

            qk_toks = {}
            for which, dest, bci in (("q", Qp, pc), ("k", Kp, 8 + pc)):
                coff = 0 if which == "q" else 128
                for dc in range(NDC):
                    for tt in range(NQ):
                        b = banks[tt]
                        w = op("tensor",
                               lambda e, b=b, dc=dc, tt=tt, coff=coff:
                               e.matmul(b.t[:, :], Wqk[:, dc, coff:coff + 128],
                                        x_fm[:, dc, 512 * tt:512 * tt + 512],
                                        start=(dc == 0), stop=(dc == NDC - 1)),
                               reads=([t_wqk, ("ve_xfm", 8 * (tt + 1))] + b.rw())
                                     if dc == 0 else (),
                               inc="pe_qk" if dc == NDC - 1 else None)
                        if dc == NDC - 1:
                            b.w = w
                for tt in range(NQ):
                    b = banks[tt]
                    extra = [("pe_s", 80 * pc)] if (pc >= 1 and stop_after >= 1.6) else []
                    b.r = op("vector",
                             lambda e, b=b, tt=tt, dest=dest, bci=bci:
                             e.tensor_scalar(dest[:, 512 * tt:512 * tt + 512], b.t[:, :],
                                             bqk_cols[:, bci:bci + 1], None, ALU.add),
                             reads=[b.w] + extra, inc="ve_qk")
                    qk_toks[(which, tt)] = b.r
                feed(NDC * NQ)
            t_qk = tok("ve_qk")
            if pc == 0:
                d_q0 = mkdump("q0", [128, S])
                if d_q0 is not None:
                    op("gpsimd", lambda e, d_q0=d_q0: e.dma_start(d_q0, Qp[:, :]),
                       reads=[t_qk], inc="st_dbg", inc_n=16)
                d_k0 = mkdump("k0", [128, S])
                if d_k0 is not None:
                    op("gpsimd", lambda e, d_k0=d_k0: e.dma_start(d_k0, Kp[:, :]),
                       reads=[t_qk], inc="st_dbg", inc_n=16)
                d_v0 = mkdump("v0", [128, NT * 264])
                if d_v0 is not None:
                    op("gpsimd", lambda e, d_v0=d_v0: e.dma_start(d_v0, V2[0][:, :]),
                       reads=[t_vg], inc="st_dbg", inc_n=16)
            if stop_after < 1.6:
                continue

            tiles = []
            for Q in range(NQ):
                nki = 4 * Q + 4
                for ki in range(nki):
                    for head in range(2):
                        tiles.append((Q, ki, head, nki))
            NTIL = len(tiles)
            L = 2
            p_rdy = [None] * NTIL

            def emit_evac(pc, Q, CA, CB, delay=DELAY_FOODS):
                e_i = ev_idx[0]
                ev_idx[0] += 1
                op("vector", lambda e, CA=CA: e.tensor_copy(ddA[:, :], CA.t[64:65, :]),
                   reads=[CA.w, ("pe_rd", 2 * e_i), t_consts], inc="ve_dd")
                t_db = op("vector", lambda e, CB=CB: e.tensor_copy(ddB[:, :], CB.t[64:65, :]),
                          reads=[CB.w], force=[tok("ve_dd")], inc="ve_dd")
                t_r = op("vector", lambda e: e.reciprocal(ddR[:, :], ddR[:, :]),
                         force=[t_db], inc="ve_rd")

                def batch(pc=pc, Q=Q, CA=CA, CB=CB, t_r=t_r, e_i=e_i):
                    rb = banks[(cur_t[0] + 2) % 4]
                    rb.w = op("tensor",
                              lambda e, rb=rb: e.matmul(rb.t[:, :], ones33AB[:, :],
                                                        ddR[:, :], start=True, stop=True),
                              reads=[t_r] + rb.rw(), inc="pe_rd", inc_n=2)
                    t_rep = op("vector", lambda e, rb=rb: e.tensor_copy(rd_rep[:, :], rb.t[:, :]),
                               reads=[rb.w], inc="ve_rd")
                    rb.r = t_rep
                    cs = e_i % 2
                    war_aw = [("pe_aw", e_i - 1)] if e_i >= 2 else []
                    op("vector",
                       lambda e, CA=CA, cs=cs:
                       e.tensor_tensor(C32s[cs][0:64, :], CA.t[0:64, :], rd_rep[0:64, :],
                                       ALU.mult),
                       force=[t_rep], reads=war_aw, inc="ve_c")
                    t_c32 = op("vector",
                               lambda e, CB=CB, cs=cs:
                               e.tensor_tensor(C32s[cs][64:128, :], CB.t[0:64, :],
                                               rd_rep[64:128, :], ALU.mult),
                               inc="ve_c")
                    CA.r = t_c32
                    CB.r = t_c32
                    op("vector",
                       lambda e, cs=cs, pc=pc, Q=Q:
                       e.tensor_copy(C_sb[0:64, pc, 512 * Q:512 * Q + 512], C32s[cs][0:64, :]),
                       force=[t_c32], inc="ve_c")
                    t_cpb = op("vector",
                               lambda e, cs=cs, pc=pc, Q=Q:
                               e.tensor_copy(C_sb[64:128, pc, 512 * Q:512 * Q + 512],
                                             C32s[cs][64:128, :]),
                               inc="ve_c")
                    t_clo = op("vector",
                               lambda e, cs=cs, pc=pc, Q=Q:
                               e.tensor_tensor(C_lo16s[cs][:, :], C32s[cs][:, :],
                                               C_sb[:, pc, 512 * Q:512 * Q + 512],
                                               ALU.subtract),
                               force=[t_cpb], inc="ve_c")
                    if pending_aw:
                        emit_aw(pending_aw.pop(0), rb)
                    pending_aw.append((cs, pc, Q, t_clo))
                defer(delay, batch)

            n_nt = NTIL // 2
            for i_nt in range(n_nt + 2):
                if i_nt < n_nt:
                    pr = i_nt % 2
                    pp = i_nt % 3
                    Q, ki, _h, nki = tiles[2 * i_nt]
                    diag = ki >= 4 * Q
                    js = 128 * (ki - 4 * Q) if diag else 0
                    for t in (2 * i_nt, 2 * i_nt + 1):
                        cur_t[0] = t
                        head = t & 1
                        sbank = banks[t % 4]
                        koff = 64 * head
                        sbank.w = op("tensor",
                                     lambda e, sbank=sbank, ki=ki, Q=Q, koff=koff:
                                     e.matmul(sbank.t[:, :],
                                              Kp[koff:koff + 64, 128 * ki:128 * ki + 128],
                                              Qp[koff:koff + 64, 512 * Q:512 * Q + 512],
                                              start=True, stop=True, tile_position=(koff, 0)),
                                     reads=[qk_toks[("q", Q)], qk_toks[("k", ki // 4)]]
                                           + sbank.rw(), inc="pe_s")
                        feed()
                    sw0 = banks[2 * pr].w
                    sw1 = banks[2 * pr + 1].w
                    pf = p_free[pp]
                    if not diag:
                        t_exp = op("scalar",
                                   lambda e, pr=pr, pp=pp:
                                   e.activation(P_sb[pp][:, :], dpair[pr][:, :],
                                                AF.Exp, bias=shift_col[:, 0:1], scale=0.125),
                                   reads=[sw0, sw1, pf, t_consts], inc="ae_exp")
                        banks[2 * pr].r = t_exp
                        banks[2 * pr + 1].r = t_exp
                        p_rdy[2 * i_nt] = t_exp
                        p_rdy[2 * i_nt + 1] = t_exp
                    else:
                        for head in range(2):
                            t_exp = op("scalar",
                                       lambda e, pr=pr, pp=pp, js=js, head=head:
                                       e.activation(P_sb[pp][:, 512 * head + js:512 * head + 512],
                                                    dpair[pr][:, 512 * head + js:512 * head + 512],
                                                    AF.Exp, bias=shift_col[:, 0:1], scale=0.125),
                                       reads=[sw0, sw1, pf, t_consts], inc="ae_exp")
                            banks[2 * pr + head].r = t_exp
                            p_rdy[2 * i_nt + head] = op(
                                "vector",
                                lambda e, pp=pp, js=js, head=head:
                                e.tensor_tensor(P_sb[pp][:, 512 * head + js:512 * head + js + 128],
                                                P_sb[pp][:, 512 * head + js:512 * head + js + 128],
                                                mask_tri[:, :], ALU.mult),
                                reads=[t_exp, t_consts], inc="ge_mask")
                if stop_after < 1.7 or i_nt < 2:
                    continue
                for j in (2 * (i_nt - 2), 2 * (i_nt - 2) + 1):
                    Qj, kij, headj, nkij = tiles[j]
                    CA = banks[4 + 2 * (Qj % 2)]
                    CB = banks[5 + 2 * (Qj % 2)]
                    cb = CA if headj == 0 else CB
                    jsj = 128 * (kij - 4 * Qj) if kij >= 4 * Qj else 0
                    vbase = kij * 264 + 132 * (pc % 2) + 66 * headj
                    reads_av = [p_rdy[j], t_vg]
                    if kij == 0:
                        reads_av += cb.rw()
                    w = op("tensor",
                           lambda e, cb=cb, vbase=vbase, pj=((j // 2) % 3), hh=headj,
                           js=jsj, ki=kij, nki=nkij, buf=buf:
                           e.matmul(cb.t[0:66, js:512], V2[buf][:, vbase:vbase + 66],
                                    P_sb[pj][:, 512 * hh + js:512 * hh + 512],
                                    start=(ki == 0), stop=(ki == nki - 1)),
                           reads=reads_av, inc="pe_av")
                    feed()
                    p_free[(j // 2) % 3] = w
                    if kij == nkij - 1:
                        cb.w = w
                        if headj == 1 and stop_after >= 1.8:
                            emit_evac(pc, Qj, CA, CB,
                                      64 if Qj == NQ - 1 else DELAY_FOODS)
    Wout_bf = A.view_at(attn_mark, [128, NDC * DM], F16).rearrange(
        "p (a b) -> p a b", b=DM)
    wo_r = I["W_out"].rearrange("(fc p) o -> p fc o", p=128)
    op("gpsimd", lambda e: e.dma_start(Wout_bf[:, :, :], wo_r),
       reads=[("pe_v", 64), ("pe_av", 480)], inc="ld_wo", inc_n=16)
    drain_deferred()
    if pending_aw:
        emit_aw(pending_aw.pop(0), banks[(cur_t[0] + 2) % 4])
    t_aw = tok("ve_aw") if "ve_aw" in ctx.counts else None
    t_sprep = u_toks["t_sprep"]

    d_call = mkdump("call", [128, NDC * S])
    if d_call is not None:
        op("gpsimd", lambda e: e.dma_start(d_call.rearrange("p (a b) -> p a b", b=S),
                                           C_sb[:, :, :]),
           reads=[tok("ve_c")], inc="st_dbg", inc_n=16)
    d_c0 = mkdump("c0", [128, S])
    if d_c0 is not None:
        op("gpsimd", lambda e: e.dma_start(d_c0, C_sb[:, 0, :]),
           reads=[("ve_c", 16)], inc="st_dbg", inc_n=16)
    t_attn = [tok("ve_c"), tok("pe_s"), tok("pe_av"), tok("ae_exp"), tok("ge_mask"),
              tok("pe_rd"), tok("ve_rd"), tok("ve_qk"), tok("pe_qk"), tok("pe_v"),
              tok("ve_v"), tok("pe_aw"), tok("ve_aw")]

    if stop_after <= 2:
        return early_tail()

    A.release(attn_mark)
    t_ldwo = tok("ld_wo")

    for tt in range(NT):
        ub = [banks[tt % 2], banks[2 + tt % 2]]
        for half in range(2):
            b = ub[half]
            for fc in range(NDC):
                w = op("tensor",
                       lambda e, b=b, fc=fc, tt=tt, half=half:
                       e.matmul(b.t[:, :], C_sb[:, fc, 128 * tt:128 * tt + 128],
                                Wout_bf[:, fc, 512 * half:512 * half + 512],
                                start=(fc == 0), stop=(fc == NDC - 1)),
                       reads=([t_ldwo] + t_attn + b.rw()) if fc == 0 else (),
                       inc="pe_wo" if fc == NDC - 1 else None)
            b.w = w
        for half in range(2):
            b = ub[half]
            te = tmp_e[half]
            t_te = op("vector",
                      lambda e, b=b, te=te, half=half:
                      e.tensor_tensor(te[:, :], b.t[:, :],
                                      bout_bc[:, 512 * half:512 * half + 512],
                                      ALU.add),
                      reads=[b.w], inc="ve_te")
            b.r = op("vector",
                     lambda e, te=te, tt=tt, half=half:
                     e.tensor_tensor(x_tm[:, tt, 512 * half:512 * half + 512], te[:, :],
                                     x_tm[:, tt, 512 * half:512 * half + 512], ALU.add),
                     force=[t_te], inc="ve_wo")
        op("scalar",
           lambda e, tt=tt: e.activation(xsq[:, :], x_tm[:, tt, :], AF.Square,
                                         accum_out=stats_q[:, tt:tt + 1]),
           reads=[("ve_wo", 2 * (tt + 1))] + ([t_sprep] if tt == 0 else []),
           force=[tok("ae_ln1")], inc="ae_ln1")
        op("vector",
           lambda e, tt=tt: e.reduce_sum(stats_s[:, tt:tt + 1], x_tm[:, tt, :], AX.X),
           force=[("ve_wo", 2 * (tt + 1))], inc="ve_ln1")

    def ln_stats(sem_a, sem_v):
        t1 = op("vector", lambda e: e.tensor_scalar(stats_m[:, :], stats_s[:, :], 1.0 / DM,
                                                    None, ALU.mult),
                reads=[(sem_a, 16)], force=[(sem_v, 16)], inc=sem_v)
        t2 = op("vector", lambda e: e.tensor_tensor(stats_t[:, :], stats_m[:, :], stats_m[:, :],
                                                    ALU.mult),
                force=[t1], inc=sem_v)
        t3 = op("vector", lambda e: e.scalar_tensor_tensor(stats_r[:, :], stats_q[:, :], 1.0 / DM,
                                                           stats_t[:, :], ALU.mult, ALU.subtract),
                force=[t2], inc=sem_v)
        t4 = op("scalar", lambda e: e.activation(stats_t[:, :], stats_r[:, :], AF.Sqrt,
                                                 bias=eps_col[:, 0:1], scale=1.0),
                force=[t3], inc=sem_a)
        t5 = op("vector", lambda e: e.reciprocal(stats_r[:, :], stats_t[:, :]),
                force=[t4], inc=sem_v)
        return t5

    t_rstd1 = ln_stats("ae_ln1", "ve_ln1")
    d_xres = mkdump("xres", [S, DM])
    t_xres_dump = None
    if d_xres is not None:
        drx = d_xres.rearrange("(a p) d -> p a d", p=128)
        for g in range(4):
            t_xres_dump = op("sync", lambda e, g=g, drx=drx: e.dma_start(
                drx[:, 4 * g:4 * g + 4, :], x_tm[:, 4 * g:4 * g + 4, :]),
                reads=[("ve_wo", 32)], inc="st_dbg", inc_n=16)
    t_r2 = op("vector", lambda e: e.tensor_scalar(stats_t[:, :], stats_r[:, :], 2.0,
                                                  None, ALU.mult),
              force=[t_rstd1], inc="ve_x1")
    for tt in range(NT):
        op("vector",
           lambda e, tt=tt: e.tensor_scalar(x_tm[:, tt, :], x_tm[:, tt, :],
                                            stats_m[:, tt:tt + 1], stats_t[:, tt:tt + 1],
                                            ALU.subtract, ALU.mult),
           force=[t_r2] if tt == 0 else (),
           reads=[t_xres_dump] if (tt == 0 and t_xres_dump is not None) else (),
           inc="ve_x1")
    t_x1 = tok("ve_x1")

    d_x1 = mkdump("x1", [S, DM])
    if d_x1 is not None:
        dr = d_x1.rearrange("(a p) d -> p a d", p=128)
        for g in range(4):
            op("sync", lambda e, g=g, dr=dr: e.dma_start(dr[:, 4 * g:4 * g + 4, :],
                                                         x_tm[:, 4 * g:4 * g + 4, :]),
               reads=[t_x1], inc="st_dbg", inc_n=16)
    t_x1dump = tok("st_dbg") if "x1" in debug else None

    if stop_after <= 3:
        return early_tail()

    d_aw = mkdump("aw", [1, S])
    if d_aw is not None:
        op("sync", lambda e: e.dma_start(d_aw, aw_row[:, :]), reads=[t_aw],
           inc="st_dbg", inc_n=16)
    d_xw = mkdump("xw", [128, 16])
    if d_xw is not None:
        op("sync", lambda e: e.dma_start(d_xw, xw_cols[:, :]), reads=[t_sprep],
           inc="st_dbg", inc_n=16)
    b3 = banks[3]
    for k in range(16):
        w = op("tensor",
               lambda e, b3=b3, k=k:
               e.transpose(b3.t[:, k:k + 1], aw_row[0:1, 128 * k:128 * k + 128],
                           ident_f[0:1, 0:1]),
               reads=([t_aw] + b3.rw()) if k == 0 else (),
               inc="pe_rt" if k == 15 else None)
        if k == 15:
            b3.w = w
    b3.r = op("vector", lambda e: e.tensor_copy(aw_cols[:, :], b3.t[:, 0:16]),
              reads=[b3.w], inc="ve_rt")
    s1 = op("vector", lambda e: e.tensor_add(s_cols[:, :], xw_cols[:, :], aw_cols[:, :]),
            force=[b3.r, t_rstd1], reads=[t_sprep], inc="ve_rt")
    s2 = op("vector", lambda e: e.tensor_scalar(s_cols[:, :], s_cols[:, :], bw_sw[:, 0:1],
                                                None, ALU.add),
            force=[s1], inc="ve_rt")
    s3 = op("vector", lambda e: e.scalar_tensor_tensor(s_cols[:, :], stats_m[:, :],
                                                       nsw_col[:, 0:1], s_cols[:, :],
                                                       ALU.mult, ALU.add),
            force=[s2], inc="ve_rt")
    t_scols = op("vector", lambda e: e.tensor_tensor(s_cols[:, :], s_cols[:, :],
                                                     stats_r[:, :], ALU.mult),
                 force=[s3], inc="ve_rt")

    A.release(csb_mark)
    Wq1 = [A.alloc(f"Wq1_{i}", [128, NDC, 1024], F16) for i in range(2)]
    Wq2 = [A.alloc(f"Wq2_{i}", [128, 8, 1024], F16) for i in range(2)]

    def wq_load(q):
        if q == 0:
            reads = [tok("pe_wo"), tok("ve_wo"), ("pe_rt", 1)]
        elif q < 2:
            reads = []
        else:
            reads = [("pe_u", 32 * (q - 1))]
        s1r = I["W1"][:, 1024 * q:1024 * q + 1024].rearrange("(dc p) c -> p dc c", p=128)
        s2r = I["W2"][1024 * q:1024 * q + 1024, :].rearrange("(a p) c -> p a c", p=128)
        op("gpsimd", lambda e, q=q, s1r=s1r: e.dma_start(Wq1[q % 2][:, :, :], s1r),
           reads=reads, force=[("ld_q", 32 * q)], inc="ld_q", inc_n=16)
        op("gpsimd", lambda e, q=q, s2r=s2r: e.dma_start(Wq2[q % 2][:, :, :], s2r),
           force=[("ld_q", 32 * q + 16)], inc="ld_q", inc_n=16)

    wq_load(0)
    wq_load(1)
    for dc in range(NDC):
        for g in range(4):
            b = banks[(dc * 4 + g) % 4]
            for k in range(4):
                ttt = 4 * g + k
                w = op("tensor",
                       lambda e, b=b, k=k, ttt=ttt, dc=dc:
                       e.transpose(b.t[:, 128 * k:128 * k + 128],
                                   x_tm[:, ttt, 128 * dc:128 * dc + 128], ident_f[:, :]),
                       reads=([t_x1] + b.rw()) if k == 0 else (),
                       inc="pe_x1t" if k == 3 else None)
                if k == 3:
                    b.w = w
            b.r = op("vector",
                     lambda e, b=b, dc=dc, g=g:
                     e.tensor_copy(x_fm[:, dc, 512 * g:512 * g + 512], b.t[:, :]),
                     reads=[b.w], inc="ve_x1t")
    t_x1fm = tok("ve_x1t")
    t_x1t_pe = tok("pe_x1t")

    rank_mark = A.mark()
    sT_sb = A.alloc("sT_sb", [16, 128], F32)
    s_bc = A.alloc("s_bc", [128, S], F32)
    s_row = A.alloc("s_row", [1, S], F32)
    cmp0 = A.view_at(TMPE_START, [128, S], F16)
    cmp = [cmp0, cmp0]
    rank_row = s_bc[0:1, 0:S]

    b0 = banks[0]
    b0.w = op("tensor", lambda e: e.transpose(b0.t[0:16, 0:128], s_cols[:, :], ident_f[:, :]),
              reads=[t_scols] + b0.rw(), inc="pe_rk")
    b0.r = op("vector", lambda e: e.tensor_copy(sT_sb[:, :], b0.t[0:16, 0:128]),
              reads=[b0.w], inc="ve_rk")
    t_srow = op("sync", lambda e: e.dma_start(
        s_row[0:1, 0:S].rearrange("p (a b) -> p a b", b=128), sT_sb[:, :]),
        reads=[b0.r, tok("pe_rt")], inc="ld_srow", inc_n=16)
    for j in range(NQ):
        b = banks[j]
        b.w = op("tensor",
                 lambda e, b=b, j=j: e.matmul(b.t[:, :], ones1f[0:1, :],
                                              s_row[0:1, 512 * j:512 * j + 512],
                                              start=True, stop=True),
                 reads=[t_srow] + b.rw(), inc="pe_rk")
        b.r = op("vector",
                 lambda e, b=b, j=j: e.tensor_copy(s_bc[:, 512 * j:512 * j + 512], b.t[:, :]),
                 reads=[b.w], inc="ve_rk")
    t_sbc = tok("ve_rk")

    for k in range(16):
        reads = [("pe_rka", 4 * k)] if k >= 1 else []
        op("vector",
           lambda e, k=k: e.tensor_scalar(cmp[k % 2][:, :], s_bc[:, :],
                                          s_cols[:, k:k + 1], None, ALU.is_lt),
           reads=reads, force=[t_sbc] if k == 0 else (), inc="ve_cmp")
        for j in range(NQ):
            b = banks[4 + j]
            w = op("tensor",
                   lambda e, b=b, k=k, j=j:
                   e.matmul(b.t[0:1, :], onescol_bf[:, 0:1],
                            cmp[k % 2][:, 512 * j:512 * j + 512],
                            start=(k == 0), stop=(k == 15)),
                   reads=[("ve_cmp", k + 1)] + (b.rw() if k == 0 else []),
                   inc="pe_rka")
            if k == 15:
                b.w = w
    for j in range(NQ):
        b = banks[4 + j]
        b.r = op("vector",
                 lambda e, b=b, j=j: e.tensor_copy(rank_row[:, 512 * j:512 * j + 512],
                                                   b.t[0:1, :]),
                 reads=[b.w], inc="ve_rk")
    t_rrow = tok("ve_rk")
    b1b = banks[1]
    for k in range(16):
        w = op("tensor",
               lambda e, b1b=b1b, k=k:
               e.transpose(b1b.t[:, k:k + 1], rank_row[0:1, 128 * k:128 * k + 128],
                           ident_f[0:1, 0:1]),
               reads=([t_rrow] + b1b.rw()) if k == 0 else (),
               inc="pe_rk" if k == 15 else None)
        if k == 15:
            b1b.w = w
    t_flag = op("vector",
                lambda e: e.tensor_scalar(flag_cols[:, :], b1b.t[:, 0:16], 1023.5, None,
                                          ALU.is_lt),
                reads=[b1b.w], inc="ve_rk")
    b1b.r = t_flag
    t_sig = op("scalar", lambda e: e.activation(sig_cols[:, :], s_cols[:, :], AF.Sigmoid),
               reads=[t_scols], inc="ae_rk")
    t_gate = op("vector", lambda e: e.tensor_tensor(gate_cols[:, :], flag_cols[:, :],
                                                    sig_cols[:, :], ALU.mult),
                force=[t_flag, t_sig], inc="ve_rk")

    d_s = mkdump("s", [128, 16])
    if d_s is not None:
        op("sync", lambda e: e.dma_start(d_s, s_cols[:, :]), reads=[t_scols],
           inc="st_dbg", inc_n=16)
    d_gate = mkdump("gate", [128, 16])
    if d_gate is not None:
        op("sync", lambda e: e.dma_start(d_gate, gate_cols[:, :]), reads=[t_gate],
           inc="st_dbg", inc_n=16)

    if stop_after <= 4:
        return early_tail()

    A.release(rank_mark)
    t_rank_readers = [tok("pe_rka"), tok("pe_rk"), tok("ve_rk"), tok("ve_cmp")]
    h_sb = A.alloc("h_sb", [128, 2, 8 * 512], F16)

    W2BANKS = {0: (2, 3), 1: (4, 5), 2: (6, 7), 3: (2, 3)}

    def emit_hstep(step, f8):
        q, tg = step // 4, step % 4
        gfc = 8 * q + f8
        idx = 8 * step + f8
        hb = banks[idx % 2]
        for dc in range(NDC):
            w = op("tensor",
                   lambda e, hb=hb, dc=dc, tg=tg, q=q, f8=f8:
                   e.matmul(hb.t[:, :], Wq1[q % 2][:, dc, 128 * f8:128 * f8 + 128],
                            x_fm[:, dc, 512 * tg:512 * tg + 512],
                            start=(dc == 0), stop=(dc == NDC - 1)),
                   reads=([("ld_q", 32 * (q + 1)), t_x1fm] + hb.rw())
                         if dc == 0 else (),
                   inc="pe_h" if dc == NDC - 1 else None)
        hb.w = w
        hreads = [hb.w]
        if idx == 0:
            hreads += t_rank_readers
        if step >= 2:
            hreads.append(("pe_u", 8 * (step - 1)))
        hb.r = op("scalar",
                  lambda e, hb=hb, step=step, f8=f8, gfc=gfc:
                  e.activation(h_sb[:, step % 2, 512 * f8:512 * f8 + 512], hb.t[:, :],
                               AF.Gelu_apprx_tanh,
                               bias=b1_cols[:, gfc:gfc + 1], scale=0.5),
                  reads=hreads, inc="ge_h")

    def emit_w2group(step, TT):
        q = step // 4
        bpair = [banks[W2BANKS[TT][0]], banks[W2BANKS[TT][1]]]
        for f8 in range(8):
            for half in range(2):
                b = bpair[half]
                reads = []
                if f8 == 0:
                    reads = [("ge_h", 8 * (step + 1)), ("ld_q", 32 * (q + 1))] + b.rw()
                w = op("tensor",
                       lambda e, b=b, f8=f8, TT=TT, half=half, q=q, step=step:
                       e.matmul(b.t[:, :],
                                h_sb[:, step % 2, 512 * f8 + 128 * TT:512 * f8 + 128 * TT + 128],
                                Wq2[q % 2][:, f8, 512 * half:512 * half + 512],
                                start=(f8 == 0), stop=(f8 == 7)),
                       reads=reads, inc="pe_u" if f8 == 7 else None)
                if f8 == 7:
                    b.w = w

    def emit_ffn_evac(step, TT, first):
        q, tg = step // 4, step % 4
        ttg = 4 * tg + TT
        bpair = [banks[W2BANKS[TT][0]], banks[W2BANKS[TT][1]]]
        for half in range(2):
            b = bpair[half]
            gc = gate_cols[:, ttg:ttg + 1]
            xs = (ttg, 512 * half)
            rd = [b.w]
            fo = []
            if first and half == 0:
                rd += [t_x1t_pe]
                fo += [t_gate, t_scols, t_x1]
                if t_x1dump is not None:
                    rd.append(t_x1dump)
            if q == 0:
                te = tmp_e[half]
                t_te = op("vector",
                          lambda e, b=b, te=te, half=half:
                          e.tensor_tensor(te[:, :], b.t[:, :],
                                          b2_bc[:, 512 * half:512 * half + 512], ALU.add),
                          reads=rd, force=fo, inc="ve_te")
                b.r = op("vector",
                         lambda e, te=te, ttg=ttg, half=half, gc=gc:
                         e.scalar_tensor_tensor(x_tm[:, ttg, 512 * half:512 * half + 512],
                                                te[:, :], gc,
                                                x_tm[:, ttg, 512 * half:512 * half + 512],
                                                ALU.mult, ALU.add),
                         force=[t_te], inc="ve_y")
            else:
                b.r = op("vector",
                         lambda e, b=b, ttg=ttg, half=half, gc=gc:
                         e.scalar_tensor_tensor(x_tm[:, ttg, 512 * half:512 * half + 512],
                                                b.t[:, :], gc,
                                                x_tm[:, ttg, 512 * half:512 * half + 512],
                                                ALU.mult, ALU.add),
                         reads=rd, force=fo, inc="ve_y")

    first_evac = [True]
    for step in range(17):
        if step < 16:
            q, tg = step // 4, step % 4
            if tg == 0 and q + 2 < 4:
                wq_load(q + 2)
        if step == 0:
            for f8 in range(8):
                emit_hstep(0, f8)
            continue
        pstep = step - 1
        seq = [("w2", 0), ("h", 0), ("h", 1),
               ("w2", 1), ("h", 2), ("h", 3), ("ev", 0),
               ("w2", 2), ("h", 4), ("h", 5), ("ev", 1),
               ("w2", 3), ("h", 6), ("h", 7), ("ev", 2), ("ev", 3)]
        for kind, i in seq:
            if kind == "h":
                if step < 16:
                    emit_hstep(step, i)
            elif kind == "w2":
                emit_w2group(pstep, i)
            else:
                emit_ffn_evac(pstep, i, first_evac[0])
                first_evac[0] = False

    if stop_after <= 5:
        return early_tail()

    d_y = mkdump("y", [S, DM])
    t_ydump = None
    if d_y is not None:
        dry = d_y.rearrange("(a p) d -> p a d", p=128)
        for g in range(4):
            t_ydump = op("sync", lambda e, g=g, dry=dry: e.dma_start(
                dry[:, 4 * g:4 * g + 4, :], x_tm[:, 4 * g:4 * g + 4, :]),
                reads=[("ve_y", 128)], inc="st_dbg", inc_n=16)

    out_r = out_ap.rearrange("(a p) d -> p a d", p=128)
    for gL in range(4):
        sl = slice(4 * gL, 4 * gL + 4)
        for tt in range(4 * gL, 4 * gL + 4):
            vy = 96 + 8 * gL + 2 * (tt % 4) + 2
            op("scalar",
               lambda e, tt=tt: e.activation(xsq[:, :], x_tm[:, tt, :], AF.Square,
                                             accum_out=stats_q[:, tt:tt + 1]),
               reads=[("ve_y", vy)] + ([t_ydump] if t_ydump is not None else [])
                     + ([("ae_ln1", 16)] if (gL == 0 and tt == 0) else []),
               force=[tok("ae_ln2")], inc="ae_ln2")
            op("vector",
               lambda e, tt=tt: e.reduce_sum(stats_s[:, tt:tt + 1], x_tm[:, tt, :], AX.X),
               force=[("ve_y", vy)], inc="ve_ln2")
        t1 = op("vector", lambda e, sl=sl: e.tensor_scalar(stats_m[:, sl], stats_s[:, sl],
                                                           1.0 / DM, None, ALU.mult),
                reads=[("ae_ln2", 4 * (gL + 1))], force=[("ve_ln2", 4 * (gL + 1))],
                inc="ve_ln2x")
        t2 = op("vector", lambda e, sl=sl: e.tensor_tensor(stats_t[:, sl], stats_m[:, sl],
                                                           stats_m[:, sl], ALU.mult),
                force=[t1], inc="ve_ln2x")
        t3 = op("vector", lambda e, sl=sl: e.scalar_tensor_tensor(stats_r[:, sl],
                                                                  stats_q[:, sl], 1.0 / DM,
                                                                  stats_t[:, sl], ALU.mult,
                                                                  ALU.subtract),
                force=[t2], inc="ve_ln2x")
        t4 = op("scalar", lambda e, sl=sl: e.activation(stats_t[:, sl], stats_r[:, sl],
                                                        AF.Sqrt, bias=eps_col[:, 0:1],
                                                        scale=1.0),
                reads=[("ae_ln2", 4 * (gL + 1))], force=[t3], inc="ae_ln2x")
        t5 = op("vector", lambda e, sl=sl: e.reciprocal(stats_r[:, sl], stats_t[:, sl]),
                force=[t4], inc="ve_ln2x")
        for tt in range(4 * gL, 4 * gL + 4):
            t_n = op("vector",
                     lambda e, tt=tt: e.tensor_scalar(x_tm[:, tt, :], x_tm[:, tt, :],
                                                      stats_m[:, tt:tt + 1],
                                                      stats_r[:, tt:tt + 1],
                                                      ALU.subtract, ALU.mult),
                     force=[t5] if tt == 4 * gL else (), inc="ve_out")
            op("sync", lambda e, tt=tt, out_r=out_r: e.dma_start(out_r[:, tt, :],
                                                                 x_tm[:, tt, :]),
               reads=[t_n], inc="st_out", inc_n=16)

    t_st = tok("st_out")
    t_dbg = tok("st_dbg") if "st_dbg" in ctx.counts else None

    def final(e, t_st=t_st, t_dbg=t_dbg):
        e.wait_ge(ctx.sems["st_out"], t_st[1])
        if t_dbg is not None and t_dbg[1] > 0:
            e.wait_ge(ctx.sems["st_dbg"], t_dbg[1])
    ctx.q["sync"].append(final)

    with nc.Block() as blk:
        @blk.tensor
        def _(eng):
            for f in ctx.q["tensor"]:
                f(eng)

        @blk.vector
        def _(eng):
            for f in ctx.q["vector"]:
                f(eng)

        @blk.scalar
        def _(eng):
            for f in ctx.q["scalar"]:
                f(eng)

        @blk.gpsimd
        def _(eng):
            for f in ctx.q["gpsimd"]:
                f(eng)

        @blk.sync
        def _(eng):
            for f in ctx.q["sync"]:
                f(eng)


    return nc, sorted(dumps.keys())


_CACHE = {}


def kernel(L0_attn_encodings, L0_attn_W_attn, L0_attn_b_attn, L0_attn_W_out,
           L0_attn_b_out, L0_ffn_w_router, L0_ffn_W1, L0_ffn_b1, L0_ffn_W2,
           L0_ffn_b2, _debug=()):
    key = tuple(sorted(_debug))
    if key not in _CACHE:
        _CACHE[key] = build(debug=_debug)
    nc, dump_names = _CACHE[key]

    x = np.ascontiguousarray(np.asarray(L0_attn_encodings, dtype=np.float32))
    shared = {
        "W_attn": np.ascontiguousarray(np.asarray(L0_attn_W_attn, np.float32)),
        "b_attn": np.ascontiguousarray(np.asarray(L0_attn_b_attn, np.float32)),
        "W_out": np.ascontiguousarray(np.asarray(L0_attn_W_out, np.float32)),
        "b_out": np.ascontiguousarray(np.asarray(L0_attn_b_out, np.float32)),
        "w_router": np.ascontiguousarray(np.asarray(L0_ffn_w_router, np.float32)),
        "W1": np.ascontiguousarray(np.asarray(L0_ffn_W1, np.float32)),
        "b1": np.ascontiguousarray(np.asarray(L0_ffn_b1, np.float32)),
        "W2": np.ascontiguousarray(np.asarray(L0_ffn_W2, np.float32)),
        "b2": np.ascontiguousarray(np.asarray(L0_ffn_b2, np.float32)),
    }
    in_maps = [dict(shared, x=np.ascontiguousarray(x[b])) for b in range(8)]
    res = run_bass_kernel_spmd(nc, in_maps, core_ids=list(range(8)))
    outs = np.stack([np.asarray(res.results[b]["out"]) for b in range(8)], axis=0)
    if _debug:
        dbg = [{k: np.asarray(res.results[b]["dbg_" + k]) for k in dump_names}
               for b in range(8)]
        return outs, dbg
    return outs
